# version 15
# baseline (speedup 1.0000x reference)
import math
import numpy as np
import ml_dtypes
from contextlib import ExitStack

import concourse.bass as bass
import concourse.mybir as mybir
from concourse.bass_utils import run_bass_kernel_spmd

F32 = mybir.dt.float32
BF16 = mybir.dt.bfloat16
AF = mybir.ActivationFunctionType
ALU = mybir.AluOpType
AX = mybir.AxisListType
BF = ml_dtypes.bfloat16

TS = 256
SHIFT = 12.0
SM_SCALE = 1.0 / math.sqrt(128.0)
EPS = 1e-5
ROPE_BASE = 500000.0

ENGINES = ("tensor", "vector", "scalar", "gpsimd", "sync")
T0 = ("qs", 0)


class KB:

    def __init__(self, nslot=16, dtiles=32, groups=None):
        self.groups = groups or [[0, 1], [2, 3], [4, 5], [6, 7]]
        self.NS = nslot
        self.OWN = nslot // 2
        self.DT = dtiles
        self.D = dtiles * 128
        self.NH = dtiles
        self.NKV = dtiles // 4
        self.NOT = self.NH + 2 * self.NKV
        self.DC = dtiles // 4
        self.NSUB = nslot // 4
        self.NCH = self.OWN // 2
        self.S = nslot * TS
        self.OWNT = self.OWN * TS

        self.cmds = {e: [] for e in ENGINES}
        self.cnt = {}
        self.maxw = {e: {} for e in ENGINES}
        self.sems = {}

    def _w(self, eng, waits):
        out = []
        mw = self.maxw[eng]
        for s, v in waits:
            if v <= 0 or mw.get(s, 0) >= v:
                continue
            mw[s] = v
            out.append((s, v))
        return out

    def op(self, eng, fn, waits=(), sem=None, inc=0):
        auto = ()
        if eng in ("vector", "scalar"):
            selfsem = "dve" if eng == "vector" else "act"
            c = self.cnt.get(selfsem, 0)
            if c:
                auto = ((selfsem, c),)
            if sem is None:
                sem, inc = selfsem, 1
        extra = [self.t_smalls] if hasattr(self, "t_smalls") else []
        wl = self._w(eng, list(auto) + extra + list(waits))
        if inc:
            self.cnt[sem] = self.cnt.get(sem, 0) + inc
            tok = (sem, self.cnt[sem])
        else:
            tok = None
        sems = self.sems

        def run(e, wl=wl, fn=fn, sem=sem, inc=inc):
            for s, v in wl:
                e.wait_ge(sems[s], v)
            inst = fn(e)
            if inc:
                inst.then_inc(sems[sem], inc)
        self.cmds[eng].append(run)
        return tok

    def dma(self, eng, out, in_, sem, waits=()):
        return self.op(eng, lambda e, o=out, i=in_: e.dma_start(out=o, in_=i),
                       waits=waits, sem=sem, inc=16)

    def tok(self, sem):
        return (sem, self.cnt.get(sem, 0))

    def barrier_waits(self):
        return [(s, c) for s, c in self.cnt.items()]

    def emit(self, block):
        for name in ENGINES:
            cmds = self.cmds[name]

            def runner(eng, cmds=cmds):
                for f in cmds:
                    f(eng)
            getattr(block, name)(runner)

    def alloc16(self, name, cols):
        off = self._ab
        self._ab += ((cols * 2 + 63) // 64) * 64
        assert self._ab <= self.AR_BYTES, f"SBUF overflow at {name}: {self._ab}"
        return self.ar16[:, off // 2: off // 2 + cols]

    def alloc32(self, name, cols):
        off = self._ab
        self._ab += ((cols * 4 + 63) // 64) * 64
        assert self._ab <= self.AR_BYTES, f"SBUF overflow at {name}: {self._ab}"
        return self.ar32[:, off // 4: off // 4 + cols]

    def mark(self):
        return self._ab

    def release(self, m):
        self._ab = m

    def build(self):
        nc = bass.Bass("TRN2", target_bir_lowering=False)
        self.nc = nc
        NS, DT, D = self.NS, self.DT, self.D
        NH, NKV, NOT, DC = self.NH, self.NKV, self.NOT, self.DC
        S, OWN, OWNT = self.S, self.OWN, self.OWNT

        io = {}
        io["xp"] = nc.dram_tensor("xp", [S, D], F32, kind="ExternalInput").ap()
        io["wqkvT"] = nc.dram_tensor("wqkvT", [NOT, 128, D], F32, kind="ExternalInput").ap()
        io["woT"] = nc.dram_tensor("woT", [NH, DC, 128, 512], F32, kind="ExternalInput").ap()
        io["wbT"] = nc.dram_tensor("wbT", [DC, DT, 128, 512], F32, kind="ExternalInput").ap()
        io["aT"] = nc.dram_tensor("aT", [DT, 128, 128], F32, kind="ExternalInput").ap()
        io["bmT"] = nc.dram_tensor("bmT", [DC, 128, 512], F32, kind="ExternalInput").ap()
        io["wrT"] = nc.dram_tensor("wrT", [DT, 128, 8], F32, kind="ExternalInput").ap()
        io["biasv"] = nc.dram_tensor("biasv", [1, D], F32, kind="ExternalInput").ap()
        io["qcos"] = nc.dram_tensor("qcos", [128, OWNT], F32, kind="ExternalInput").ap()
        io["qsin"] = nc.dram_tensor("qsin", [128, OWNT], F32, kind="ExternalInput").ap()
        io["kcos"] = nc.dram_tensor("kcos", [128, S], F32, kind="ExternalInput").ap()
        io["ksin"] = nc.dram_tensor("ksin", [128, S], F32, kind="ExternalInput").ap()
        io["masks"] = nc.dram_tensor("masks", [128, 4 * TS], BF16, kind="ExternalInput").ap()
        io["ident"] = nc.dram_tensor("ident", [128, 128], BF16, kind="ExternalInput").ap()
        io["onesb"] = nc.dram_tensor("onesb", [128, 128], BF16, kind="ExternalInput").ap()
        io["rmat"] = nc.dram_tensor("rmat", [8, 128], BF16, kind="ExternalInput").ap()
        io["consts"] = nc.dram_tensor("consts", [128, 8], F32, kind="ExternalInput").ap()
        io["pflag"] = nc.dram_tensor("pflag", [1, 1], mybir.dt.int32, kind="ExternalInput").ap()
        io["out"] = nc.dram_tensor("out", [OWNT, D], F32, kind="ExternalOutput").ap()

        io["k_s"] = nc.dram_tensor("k_s", [NKV, 128, S], BF16).ap()
        io["v_s"] = nc.dram_tensor("v_s", [S, NKV, 128], BF16).ap()
        io["kc_in"] = nc.dram_tensor("kc_in", [S // 1024, NKV, 128, 1024], BF16).ap()
        io["vc_in"] = nc.dram_tensor("vc_in", [S, NKV, 128], BF16).ap()
        io["kg_s"] = nc.dram_tensor("kg_s", [2, NKV, 128, 1024], BF16).ap()
        io["vg_s"] = nc.dram_tensor("vg_s", [2, 1024, NKV, 128], BF16).ap()
        io["q_s"] = nc.dram_tensor("q_s", [OWN, NH, 128, TS], BF16).ap()
        io["ao_s"] = nc.dram_tensor("ao_s", [NH, 128, OWNT], BF16).ap()
        io["x2_s"] = nc.dram_tensor("x2_s", [OWNT, D], F32).ap()
        io["wqkv_c"] = nc.dram_tensor("wqkv_c", [NOT, 128, D], BF16).ap()
        io["wo_c"] = nc.dram_tensor("wo_c", [NH, DC, 128, 512], BF16).ap()
        io["wb_c"] = nc.dram_tensor("wb_c", [DC, DT, 128, 512], BF16).ap()
        io["rc_s"] = nc.dram_tensor("rc_s", [OWN * NKV, 1024], F32).ap()
        self.io = io

        with ExitStack() as st:
            e = st.enter_context
            self.AR_BYTES = 207 * 1024
            arena = e(nc.sbuf_tensor("arena", [128, self.AR_BYTES // 2], BF16))
            self.ar16 = arena.ap()
            self.ar32 = arena.bitcast(F32).ap()
            self.ar32i = arena.bitcast(mybir.dt.int32).ap()
            self._ab = 0

            psA = e(nc.psum_tensor("psA", [128, 2048], F32))
            psB = e(nc.psum_tensor("psB", [128, 1024], F32))
            psC = e(nc.psum_tensor("psC", [128, 512], F32))
            psD = e(nc.psum_tensor("psD", [128, 512], F32))
            self.psA, self.psB = psA.ap(), psB.ap()
            self.psC, self.psD = psC.ap(), psD.ap()
            self.psC16 = psC.bitcast(BF16).ap()
            self.psD16 = psD.bitcast(BF16).ap()

            for sname in ("pe", "act", "dve", "qs", "w0", "w1",
                          "k0", "k1", "v0", "v1", "qq", "a0", "a1",
                          "x0", "x1", "sout", "swc0", "swc1", "sx2", "wg0", "wg1", "sb0", "sb1", "cc", "szr", "smg",
                          "skv0", "skv1", "svs", "sao0", "sao1", "src", "srp"):
                self.sems[sname] = e(nc.semaphore(sname))

            self._tpose_ev = {0: T0, 1: T0}
            self._tpose_ev2 = None
            self.b_onesf = self.alloc32("onesf", 128)
            self.b_ident = self.alloc16("ident", 128)
            self.b_ones = self.alloc16("ones", 128)
            self.b_rmat = self.alloc16("rmat", 128)
            self.b_masks = self.alloc16("masks", 4 * TS)
            self.b_consts = self.alloc32("consts", 16)
            self.b_pf = self.ar32i[:, (self._ab // 4):(self._ab // 4) + 1]
            self._ab += 64

            self.dma("sync", self.b_ident, io["ident"], "qs")
            self.dma("sync", self.b_ones[:, 0:128], io["onesb"], "qs")
            self.dma("sync", self.b_rmat[0:8, :], io["rmat"], "qs")
            self.dma("sync", self.b_masks, io["masks"], "qs")
            self.dma("sync", self.b_consts[:, 0:8], io["consts"], "qs")
            self.dma("sync", self.b_pf[0:1, 0:1], io["pflag"], "qs")
            self.t_smalls = self.tok("qs")
            self.op("vector", lambda e: e.tensor_copy(self.b_onesf, self.b_ones),
                    waits=[self.t_smalls])

            self.phaseA()
            self.phaseB()
            self.phaseC()

            fw = self._w("sync", [self.tok("sout")])
            sems = self.sems

            def fin(e, fw=fw):
                for s, v in fw:
                    e.wait_ge(sems[s], v)
            self.cmds["sync"].append(fin)

            block = e(nc.Block())
            self.emit(block)
        return nc

    def ln_stats(self, xin, stats, mv, sc1, waits):
        D = self.D
        nchunk = (D + 511) // 512
        for c in range(nchunk):
            lo, hi = c * 512, min(D, (c + 1) * 512)
            self.op("vector", lambda e, a=stats[:, 6 * c:6 * c + 6], b=xin[:, lo:hi]:
                    e.bn_stats(a, b), waits=(waits if c == 0 else ()))
        self.op("vector", lambda e, a=mv[:, 0:2], b=stats[:, 0:6 * nchunk]:
                e.bn_aggr(a, b))
        tv = self.tok("dve")
        self.op("scalar", lambda e, o=sc1[:, 0:1], i=mv[:, 1:2]:
                e.activation(o, i, AF.Ln, bias=self.b_consts[:, 1:2], scale=1.0),
                waits=[tv])
        ta = self.op("scalar", lambda e, o=sc1[:, 1:2], i=sc1[:, 0:1]:
                     e.activation(o, i, AF.Exp, bias=self.b_consts[:, 2:3], scale=-0.5))
        return ta

    def ln_row(self, xin, dst_bf, stats, mv, sc1, waits):
        ta = self.ln_stats(xin, stats, mv, sc1, waits)
        return self.op("vector", lambda e, o=dst_bf, i=xin, m=mv[:, 0:1], r=sc1[:, 1:2]:
                       e.tensor_scalar(o, i, m, r, ALU.subtract, ALU.mult),
                       waits=[ta])

    def transpose_tm2fm(self, src_bf, dst_grp_fn, n_dt, pe_waits, dve_waits=()):
        t = None
        dw = list(dve_waits)
        evtok = self._tpose_ev
        for gi, grp in enumerate(range(0, n_dt, 8)):
            g = min(8, n_dt - grp)
            par = gi % 2
            ps = self.psC16 if par == 0 else self.psD16
            w = list(pe_waits) + [evtok[par]]
            for j in range(g):
                self.op("tensor", lambda e, o=ps[:, j * 128:(j + 1) * 128],
                        i=src_bf[:, (grp + j) * 128:(grp + j + 1) * 128]:
                        e.transpose(o, i, self.b_ident),
                        waits=(w if j == 0 else ()),
                        sem="pe", inc=(1 if j == g - 1 else 0))
            tp = self.tok("pe")
            dst3 = dst_grp_fn(grp, g)
            src3 = ps[:, 0:g * 128].rearrange("p (j c) -> p j c", c=128)
            t = self.op("vector", lambda e, o=dst3, i=src3: e.tensor_copy(o, i),
                        waits=[tp] + dw)
            dw = []
            evtok[par] = t
        return t

    def rope(self, ps, cosb, sinb, out_bf, width, waits):
        swp, t1 = self._rope_swp, self._rope_t1
        self.op("vector", lambda e, o=swp[0:64, 0:width], i=ps[64:128, 0:width]:
                e.tensor_copy(o, i), waits=waits)
        self.op("vector", lambda e, o=swp[64:128, 0:width], i=ps[0:64, 0:width]:
                e.tensor_copy(o, i))
        self.op("vector", lambda e, o=t1[:, 0:width], i=ps[:, 0:width], c=cosb[:, 0:width]:
                e.tensor_mul(o, i, c))
        self.op("vector", lambda e, o=swp[:, 0:width], i=swp[:, 0:width], s=sinb[:, 0:width]:
                e.tensor_mul(o, i, s))
        return self.op("vector", lambda e, o=out_bf[:, 0:width], a=t1[:, 0:width],
                       b=swp[:, 0:width]: e.tensor_add(o, a, b))

    def phaseA(self):
        DT, D, NH, NKV, NOT = self.DT, self.D, self.NH, self.NKV, self.NOT
        NS = self.NS
        io = self.io
        m0 = self.mark()
        xrow = self.alloc32("xrowA", D)
        lnt = self.alloc16("lntA", D)
        stats = self.alloc32("statsA", 6 * ((D + 511) // 512))
        mv = self.alloc32("mvA", 4)
        sc1 = self.alloc32("sc1A", 4)
        ln1_fm = [self.alloc16("ln1fm", DT * 512) for _ in range(2)]
        wbuf = [self.alloc16("wbufA", D) for _ in range(2)]
        self._rope_swp = self.alloc32("ropeSwp", 512)
        self._rope_t1 = self.alloc32("ropeT1", 512)
        kq_bf = [self.alloc16("kqbf", 512) for _ in range(2)]
        vsb = self.alloc16("vsbA", 512)
        vtb = self.alloc16("vtbA", 512)
        rsc0 = self.alloc32("rscA", 512)
        rsc = [rsc0, rsc0]
        qcosb = self.alloc32("qcosA", 512)
        qsinb = self.alloc32("qsinA", 512)
        zt = self.alloc16("ztA", 2048)

        t_wcons = {0: T0, 1: T0}
        t_wcache = {0: T0, 1: T0}
        t_rowcons = T0
        t_lntcons = T0
        t_kqcons = {0: T0, 1: T0}
        t_vspill = T0
        t_psA = {0: T0, 1: T0}
        t_psB = {0: T0, 1: T0}
        t_vsb = T0
        t_qkv_pe = {0: T0, 1: T0}
        self._rope_prev = {}
        t_rope_last = T0
        t_lnfm = {}
        self.t_qkvcache = []

        tz = self.op("vector", lambda e: e.memset(zt, 0.0))
        for j in range(1, NS, 2):
            for g in range(NKV):
                self.dma("sync", io["kc_in"][j // 4, g][:, (j % 4) * TS:(j % 4 + 1) * TS],
                         zt[:, 0:TS], "szr", waits=[tz])
            self.dma("sync",
                     io["vc_in"][j * TS:(j + 1) * TS].rearrange(
                         "(kt p) g h -> p kt g h", p=128),
                     zt[:, 0:2 * NKV * 128].rearrange(
                         "p (kt g h) -> p kt g h", g=NKV, h=128),
                     "szr", waits=[tz])
        t_zero = self.tok("szr")


        v2d = io["vc_in"].rearrange("s g h -> (s g) h")
        kgo = io["kg_s"].rearrange("r g p s -> (r g p) s")
        vgo = io["vg_s"].rearrange("r s g h -> (r s g) h")
        NG = NKV * 128

        def emit_collective(p):
            self.op("gpsimd",
                    lambda e, i=io["kc_in"][p].rearrange("g p s -> (g p) s"), o=kgo:
                    e.collective_compute("AllGather", ALU.bypass,
                                         self.groups, [i], [o]),
                    waits=[self.tok("skv0"), self.tok("skv1"),
                           self.tok("svs"), self.tok("smg"), t_zero],
                    sem="cc", inc=1)
            tcc = self.op("gpsimd",
                          lambda e, i=v2d[p * 1024 * NKV:(p + 1) * 1024 * NKV, :],
                          o=vgo:
                          e.collective_compute("AllGather", ALU.bypass,
                                               self.groups, [i], [o]),
                          sem="cc", inc=1)
            kr = [io["kg_s"][r].rearrange("g p (pr two t) -> g p pr two t",
                                          two=2, t=TS) for r in range(2)]
            vr = [io["vg_s"][r].rearrange("(pr two t) g h -> pr two t g h",
                                          two=2, t=TS) for r in range(2)]
            ksl = io["k_s"][:, :, p * 1024:(p + 1) * 1024].rearrange(
                "g p (pr two t) -> g p pr two t", two=2, t=TS)
            vsl = io["v_s"][p * 1024:(p + 1) * 1024].rearrange(
                "(pr two t) g h -> pr two t g h", two=2, t=TS)
            wl = self._w("gpsimd", [tcc])
            sems = self.sems
            pf = self.b_pf

            def merge(e, wl=wl, kr=kr, vr=vr, ksl=ksl, vsl=vsl, p=p):
                for s_, v_ in wl:
                    e.wait_ge(sems[s_], v_)
                with e.register("prx%d" % p) as r_:
                    e.reg_load(r_, pf[0:1, 0:1])
                    for cond, m in ((e.If_eq, 0), (e.Else, None)):
                        ctx = cond(r_, 0) if m is not None else cond()
                        my = 0 if m is not None else 1
                        with ctx:
                            e.dma_start(out=ksl[:, :, :, 0, :],
                                        in_=kr[my][:, :, :, 0, :]).then_inc(sems["smg"], 16)
                            e.dma_start(out=ksl[:, :, :, 1, :],
                                        in_=kr[1 - my][:, :, :, 0, :]).then_inc(sems["smg"], 16)
                            e.dma_start(out=vsl[:, 0],
                                        in_=vr[my][:, 0]).then_inc(sems["smg"], 16)
                            e.dma_start(out=vsl[:, 1],
                                        in_=vr[1 - my][:, 0]).then_inc(sems["smg"], 16)
            self.cmds["gpsimd"].append(merge)
            self.cnt["smg"] = self.cnt.get("smg", 0) + 64

        def emit_tables(p):
            self.dma("sync", qcosb, io["qcos"][:, p * 512:(p + 1) * 512], "qs",
                     waits=[t_rope_last])
            self.dma("sync", qsinb, io["qsin"][:, p * 512:(p + 1) * 512], "qs")
            return self.tok("qs")

        def emit_ln_row(p, r):
            nonlocal t_rowcons, t_lntcons
            base = p * 1024 + (r // 2) * 512 + (r % 2) * 128
            trow = self.dma("sync", xrow, io["xp"][base:base + 128, :],
                            "x0", waits=[t_rowcons])
            tn = self.ln_row(xrow, lnt, stats, mv, sc1, waits=[trow, t_lntcons])
            t_rowcons = tn
            v = ln1_fm[p % 2].rearrange("p (dt t) -> p dt t", t=512)
            tk = self.transpose_tm2fm(
                lnt,
                lambda grp, g, r=r, v=v: v[:, grp:grp + g, r * 128:(r + 1) * 128],
                DT, pe_waits=[tn],
                dve_waits=([t_qkv_pe[p % 2]] if r == 0 else ()))
            t_lntcons = self.tok("pe")
            return tk

        t_tab = emit_tables(0)
        for r in range(4):
            t_lnfm[0] = emit_ln_row(0, r)

        for p in range(self.NSUB):
            fm = ln1_fm[p % 2]
            pending = list(range(4)) if p + 1 < self.NSUB else []
            step = max(1, NOT // 4)
            for ot in range(NOT):
                if ot == 2 and p > 0:
                    emit_collective(p - 1)
                wpar = ot % 2
                if p == 0:
                    tw = self.dma("gpsimd", wbuf[wpar], io["wqkvT"][ot], "wg%d" % wpar,
                                  waits=[t_wcons[wpar], t_wcache[wpar]])
                    t_wcache[wpar] = self.dma("sync", io["wqkv_c"][ot], wbuf[wpar],
                                              "swc%d" % wpar, waits=[tw])
                else:
                    tw = self.dma("scalar", wbuf[wpar], io["wqkv_c"][ot], "w%d" % wpar,
                                  waits=[t_wcons[wpar]] + self.t_qkvcache)
                is_q = ot < NH
                if is_q:
                    qpar = ot % 2
                    ps = self.psB[:, qpar * 512:(qpar + 1) * 512]
                    wps = t_psB
                    pspar = qpar
                else:
                    kv = ot - NH
                    g = kv % NKV
                    is_k = kv < NKV
                    pspar = ot % 2
                    ps = self.psA[:, pspar * 1024: pspar * 1024 + 512]
                    wps = t_psA
                for dt in range(DT):
                    self.op("tensor",
                            lambda e, o=ps, l=wbuf[wpar][:, dt * 128:(dt + 1) * 128],
                            r=fm[:, dt * 512:(dt + 1) * 512],
                            s=(dt == 0), z=(dt == DT - 1):
                            e.matmul(o, l, r, start=s, stop=z),
                            waits=([tw, t_lnfm[p], wps[pspar]] if dt == 0 else ()),
                            sem="pe", inc=(1 if dt == DT - 1 else 0))
                t_mm = self.tok("pe")
                t_wcons[wpar] = t_mm
                if is_q or is_k:
                    kqp = ot % 2
                    tec = self.op("scalar", lambda e, o=rsc[kqp][:, 0:512], i=ps:
                                  e.copy(o, i),
                                  waits=[t_mm, self._rope_prev.get(0, T0),
                                         self._rope_prev.get(1, T0)])
                    wps[pspar] = tec
                    tdone = self.rope(rsc[kqp], qcosb, qsinb, kq_bf[kqp], 512,
                                      [tec, t_kqcons[kqp], t_tab])
                    self._rope_prev[kqp] = tdone
                    t_rope_last = tdone
                    if is_q:
                        dst = io["q_s"][2 * p: 2 * p + 2, ot].rearrange("s p t -> p s t")
                    else:
                        dst = io["kc_in"][p, g].rearrange(
                            "p (pr two t) -> p pr two t", two=2, t=TS)[:, :, 0, :]
                    src3 = kq_bf[kqp].rearrange("p (s t) -> p s t", s=2)
                    t_kqcons[kqp] = self.dma("sync", dst, src3, "skv%d" % kqp,
                                             waits=[tdone])
                else:
                    tev = self.op("scalar", lambda e, o=vsb, i=ps:
                                  e.copy(o, i), waits=[t_mm, t_vsb])
                    wps[pspar] = tev
                    tvt = self.transpose_tm2fm(
                        vsb,
                        lambda grp, g_: vtb[:, grp * 128:(grp + g_) * 128]
                        .rearrange("p (j c) -> p j c", c=128),
                        4, pe_waits=[tev], dve_waits=[t_vspill])
                    t_vsb = tvt
                    for c in range(2):
                        row0 = (4 * p + 2 * c) * TS
                        dst = io["vc_in"][row0:row0 + TS, g, :].rearrange(
                            "(kt p) h -> p kt h", p=128)
                        src3 = vtb[:, c * 256:(c + 1) * 256].rearrange(
                            "p (kt h) -> p kt h", h=128)
                        t_vspill = self.dma("sync", dst, src3, "svs", waits=[tvt])
                t_qkv_pe[p % 2] = t_mm
                if pending and ot % step == step - 1:
                    t_lnfm[p + 1] = emit_ln_row(p + 1, pending.pop(0))
            if p == 0:
                self.t_qkvcache = [self.tok("swc0"), self.tok("swc1")]
            if p + 1 < self.NSUB:
                while pending:
                    t_lnfm[p + 1] = emit_ln_row(p + 1, pending.pop(0))
                t_tab = emit_tables(p + 1)
        emit_collective(self.NSUB - 1)

        self.release(m0)
        self.tA_end = self.barrier_waits()

    def phaseB(self):
        NKV, OWN = self.NKV, self.OWN
        io = self.io
        m0 = self.mark()
        qb = self.alloc16("qB", 1024)
        kb = [self.alloc16("kB", 1024) for _ in range(2)]
        vb = [self.alloc16("vB", 1024) for _ in range(2)]
        ptr = [self.alloc16("ptR", 1024) for _ in range(4)]
        ptm = [self.alloc16("ptM", 1024) for _ in range(2)]
        recip = self.alloc32("recipB", 2048)
        rep = self.alloc32("repB", 1024)
        aob = [self.alloc16("aoB", 1024) for _ in range(2)]
        psA, psB, psC, psD = self.psA, self.psB, self.psC, self.psD

        bar = list(self.tA_end)
        first = True
        t_qcons = T0
        t_kcons = {0: T0, 1: T0}
        t_vcons = {0: T0, 1: T0}
        t_aocons = {0: T0, 1: T0}
        t_pe_ao = T0
        t_ao_ev = T0
        prev_exp_last = T0

        for s in range(OWN):
            kslots = 2 * s + 2
            nkt = 2 * kslots
            nch = (kslots + 3) // 4
            for g in range(NKV):
                w0 = bar if first else []
                src = io["q_s"][s, 4 * g: 4 * g + 4].rearrange("h p t -> p h t")
                qb3 = qb.rearrange("p (h t) -> p h t", h=4)
                tq = self.dma("scalar", qb3, src, "qq", waits=list(w0) + [t_qcons])
                tkl, tvl = {}, {}

                def load_chunk(c, extra):
                    cp = c % 2
                    span = min(1024, kslots * TS - c * 1024)
                    tkl[c] = self.dma("scalar", kb[cp][:, 0:span],
                                      io["k_s"][g][:, c * 1024: c * 1024 + span],
                                      "k%d" % cp,
                                      waits=list(extra) + [t_kcons[cp]])
                    tvl[c] = self.dma("scalar",
                                      vb[cp][:, 0:span].rearrange(
                                          "p (kt h) -> p kt h", h=128),
                                      io["v_s"][c * 1024: c * 1024 + span, g, :]
                                      .rearrange("(kt p) h -> p kt h", p=128),
                                      "v%d" % cp, waits=[t_vcons[cp]])

                for c in range(min(2, nch)):
                    load_chunk(c, w0 if first else [])
                first = False
                t_exp, t_pv = {}, {}

                def emit_qk(kt):
                    c, cp = kt // 8, (kt // 8) % 2
                    loc = kt % 8
                    sreg = psA[:, (kt % 2) * 1024:(kt % 2) * 1024 + 1024]
                    w = [tkl[c], tq]
                    if kt >= 2:
                        w.append(t_exp[kt - 2])
                    elif kt == 0:
                        w.append(prev_exp_last)
                    for hp in range(2):
                        self.op("tensor",
                                lambda e, o=sreg[:, hp * 512:(hp + 1) * 512],
                                l=kb[cp][:, loc * 128:(loc + 1) * 128],
                                r=qb[:, hp * 512:(hp + 1) * 512]:
                                e.matmul(o, l, r, start=True, stop=True),
                                waits=(w if hp == 0 else ()),
                                sem="pe", inc=(1 if hp == 1 else 0))
                    tqk = self.tok("pe")
                    wv = [tqk]
                    if kt >= 2:
                        wv.append(t_pv[kt - 2])
                    t_exp[kt] = self.op("scalar",
                                        lambda e, o=ptr[kt % 4], i=sreg:
                                        e.activation(o, i, AF.Exp,
                                                     bias=self.b_consts[:, 0:1],
                                                     scale=SM_SCALE),
                                        waits=wv)
                    if kt >= nkt - 4:
                        mi = kt - (nkt - 4)
                        mk = self.b_masks[:, mi * TS:(mi + 1) * TS] \
                            .rearrange("p (o t) -> p o t", o=1).to_broadcast([128, 4, TS])
                        pin = ptr[kt % 4].rearrange("p (h t) -> p h t", h=4)
                        pot = ptm[kt % 2].rearrange("p (h t) -> p h t", h=4)
                        wm = [t_exp[kt]]
                        if kt - 2 >= nkt - 4:
                            wm.append(t_pv[kt - 2])
                        t_exp[kt] = self.op("vector",
                                            lambda e, o=pot, i=pin, m=mk:
                                            e.tensor_tensor(o, i, m, ALU.mult),
                                            waits=wm)

                def emit_pvsum(kt):
                    cp, loc = (kt // 8) % 2, kt % 8
                    psrc = ptm[kt % 2] if kt >= nkt - 4 else ptr[kt % 4]
                    wpv = [t_exp[kt], tvl[kt // 8]]
                    if kt == 0:
                        wpv = wpv + [t_pe_ao_w, t_ao_ev_w]
                    for hp in range(2):
                        self.op("tensor",
                                lambda e, o=psB[:, hp * 512:(hp + 1) * 512],
                                l=vb[cp][:, loc * 128:(loc + 1) * 128],
                                r=psrc[:, hp * 512:(hp + 1) * 512],
                                st=(kt == 0), sp=(kt == nkt - 1):
                                e.matmul(o, l, r, start=st, stop=sp),
                                waits=(wpv if hp == 0 else ()), sem="pe", inc=0)
                    for hp in range(2):
                        ps_s = psC if hp == 0 else psD
                        self.op("tensor",
                                lambda e, o=ps_s[:, 0:512],
                                l=self.b_ones[:, 0:128],
                                r=psrc[:, hp * 512:(hp + 1) * 512],
                                st=(kt == 0), sp=(kt == nkt - 1):
                                e.matmul(o, l, r, start=st, stop=sp),
                                sem="pe", inc=(1 if hp == 1 else 0))
                    t_pv[kt] = self.tok("pe")
                    if loc == 7 and (kt // 8 + 2) < nch:
                        t_kcons[cp] = t_pv[kt]
                        t_vcons[cp] = t_pv[kt]
                        load_chunk(kt // 8 + 2, [])

                t_pe_ao_w, t_ao_ev_w = t_pe_ao, t_ao_ev
                for kt in range(nkt + 1):
                    if kt < nkt:
                        emit_qk(kt)
                    if kt >= 1:
                        emit_pvsum(kt - 1)
                prev_exp_last = t_exp[nkt - 1]
                t_qcons = t_pv[nkt - 1]
                for cp in range(2):
                    t_kcons[cp] = t_pv[nkt - 1]
                    t_vcons[cp] = t_pv[nkt - 1]
                self.op("scalar", lambda e, o=recip[:, 0:512], i=psC[:, 0:512]:
                        e.activation(o, i, AF.Ln, bias=self.b_consts[:, 2:3],
                                     scale=1.0), waits=[t_pv[nkt - 1]])
                self.op("scalar", lambda e, o=recip[:, 512:1024], i=psD[:, 0:512]:
                        e.activation(o, i, AF.Ln, bias=self.b_consts[:, 2:3],
                                     scale=1.0))
                te = self.op("scalar", lambda e, o=rep[:, 0:1024], i=recip[:, 0:1024]:
                             e.activation(o, i, AF.Exp, bias=self.b_consts[:, 2:3],
                                          scale=-1.0))
                ap = (s * NKV + g) % 2
                tao = self.op("vector",
                              lambda e, o=aob[ap], a=psB, b=rep:
                              e.tensor_tensor(o, a, b, ALU.mult),
                              waits=[te, t_aocons[ap]])
                t_ao_ev = tao
                t_pe_ao = t_pv[nkt - 1]
                dst = io["ao_s"][4 * g: 4 * g + 4, :, s * TS:(s + 1) * TS] \
                    .rearrange("h p t -> p h t")
                src3 = aob[ap].rearrange("p (h t) -> p h t", h=4)
                t_aocons[ap] = self.dma("sync", dst, src3, "sao%d" % ap, waits=[tao])

        self.release(m0)
        self.tB_end = self.barrier_waits()

    def phaseC(self):
        NH, DT, DC, D, NCH = self.NH, self.DT, self.DC, self.D, self.NCH
        io = self.io
        m0 = self.mark()
        aTb = self.alloc16("aTb", DT * 128)
        bmTb = self.alloc16("bmTb", DC * 512)
        wrTb = self.alloc16("wrTb", DT * 8)
        biasb = self.alloc32("biasC", 1024)
        bar = list(self.tB_end)
        self.dma("gpsimd", aTb.rearrange("p (dt er) -> p dt er", er=128), io["aT"].rearrange("dt p er -> p dt er"), "wg0",
                 waits=bar)
        self.dma("gpsimd", bmTb.rearrange("p (oc o) -> p oc o", o=512), io["bmT"].rearrange("oc p o -> p oc o"), "wg0")
        self.dma("gpsimd", wrTb.rearrange("p (dt o) -> p dt o", o=8), io["wrT"].rearrange("dt p o -> p dt o"), "wg0")
        t_small = [self.tok("wg0")]
        t_biascons = {0: T0, 1: T0}

        aob = self.alloc16("aoC", NH * 512)
        wo_out = self.alloc32("woout", 4 * D)
        ln2_fm = aob
        ffn_out = wo_out
        wbuf = [self.alloc16("wbufC", 2048) for _ in range(2)]
        xrow = [self.alloc32("xrowC", D) for _ in range(2)]
        lnt = [self.alloc16("lntC", D) for _ in range(2)]
        x2r = self.alloc32("x2rC", D)
        stats = self.alloc32("statsC", 6 * ((D + 511) // 512))
        mv = self.alloc32("mvC", 4)
        sc1 = self.alloc32("sc1C", 4)
        rt = self.alloc32("rtC", 64)
        wrow = self.alloc32("wrowC", 8)
        wrow16 = self.alloc16("wrow16", 8)
        wtsb = self.alloc16("wtsb", 512)
        wrepb = self.alloc16("wrepC", 512)
        hsb = self.alloc32("hsbC", 512)

        psA, psB, psC, psD = self.psA, self.psB, self.psC, self.psD
        t_evA = self.tok("act")
        t_wcons = {0: T0, 1: T0}
        t_wcache = {0: T0, 1: T0}
        t_aocons = {0: T0, 1: T0}
        t_rowcons = {0: T0, 1: T0}
        t_x2w = T0
        t_wo_last = T0
        t_ccache = []
        prev_wt3 = T0
        prev_hw = T0
        t_out_prev = T0

        def emit_final_row(ch, t_ffev, r):
            nonlocal t_x2w, t_out_prev
            if True:
                par = r % 2
                frow = ffn_out[:, r * D:(r + 1) * D]
                ta = self.ln_stats(frow, stats, mv, sc1, waits=[t_ffev])
                tx2 = self.dma("sync", xrow[par],
                               io["x2_s"][(ch * 4 + r) * 128:(ch * 4 + r + 1) * 128, :],
                               "x%d" % par, waits=[t_rowcons[par], t_x2w])
                ts1 = self.op("vector",
                              lambda e, o=x2r, i=frow, m=mv[:, 0:1], rs=sc1[:, 1:2]:
                              e.tensor_scalar(o, i, m, rs, ALU.subtract, ALU.mult),
                              waits=[tx2, ta, t_out_prev])
                ts2 = self.op("vector", lambda e, o=x2r, a=x2r, b=xrow[par]:
                              e.tensor_add(o, a, b))
                t_rowcons[par] = ts2
                t_out_prev = self.dma("sync",
                                      io["out"][(ch * 4 + r) * 128:(ch * 4 + r + 1) * 128, :],
                                      x2r, "sout", waits=[ts2])

        for ch in range(NCH):
            t_aocons[0] = self.dma(
                "sync", aob.rearrange("p (h t) -> p h t", t=512),
                io["ao_s"][:, :, ch * 512:(ch + 1) * 512].rearrange("h p t -> p h t"),
                "a0", waits=(bar if ch == 0 else []) + [t_aocons[0], t_wo_last])
            t_ao_all = [t_aocons[0]]
            t_ev = t_evA
            for dc in range(DC):
                for og in range(0, NH, 4):
                    wpar = (og // 4) % 2
                    wsem = "w%d" % wpar
                    wv = wbuf[wpar].rearrange("p (o d) -> p o d", o=4)
                    if ch == 0:
                        tw = self.dma("gpsimd", wv,
                                      io["woT"][og:og + 4, dc].rearrange("o p d -> p o d"),
                                      "wg%d" % wpar, waits=[t_wcons[wpar], t_wcache[wpar]])
                        t_wcache[wpar] = self.dma(
                            "sync", io["wo_c"][og:og + 4, dc].rearrange("o p d -> p o d"),
                            wv, "swc%d" % wpar, waits=[tw])
                    else:
                        tw = self.dma("gpsimd", wv,
                                      io["wo_c"][og:og + 4, dc].rearrange("o p d -> p o d"),
                                      "wg%d" % wpar, waits=[t_wcons[wpar]] + t_ccache)
                    for oi in range(4):
                        ot = og + oi
                        for tsub in range(4):
                            self.op("tensor",
                                    lambda e, o=psA[:, tsub * 512:(tsub + 1) * 512],
                                    l=aob[:, ot * 512 + tsub * 128: ot * 512 + (tsub + 1) * 128],
                                    r=wbuf[wpar][:, oi * 512:(oi + 1) * 512],
                                    st=(ot == 0), sp=(ot == NH - 1):
                                    e.matmul(o, l, r, start=st, stop=sp),
                                    waits=(([tw] + (t_ao_all + [t_ev] if ot == 0 else []))
                                           if (tsub == 0 and oi == 0) else ()),
                                    sem="pe",
                                    inc=(1 if (tsub == 3 and oi == 3) else 0))
                    t_wcons[wpar] = self.tok("pe")
                tmm = self.tok("pe")
                for tsub in range(4):
                    self.op("scalar",
                            lambda e, o=wo_out[:, tsub * D + dc * 512: tsub * D + (dc + 1) * 512],
                            i=psA[:, tsub * 512:(tsub + 1) * 512]:
                            e.copy(o, i), waits=[tmm])
                t_ev = self.tok("act")
            t_wo_last = self.tok("pe")

            pe_guard = self.tok("pe")
            t_lnfm = None
            for r in range(4):
                par = r % 2
                grow = (4 * ch + 2 * (r // 2)) * TS + (r % 2) * 128
                trow = self.dma("sync", xrow[par], io["xp"][grow:grow + 128, :],
                                "x%d" % par, waits=[t_rowcons[par]])
                worow = wo_out[:, r * D:(r + 1) * D]
                ta = self.ln_stats(worow, stats, mv, sc1, waits=[t_ev])
                ts1 = self.op("vector",
                              lambda e, o=x2r, i=worow, m=mv[:, 0:1], rs=sc1[:, 1:2]:
                              e.tensor_scalar(o, i, m, rs, ALU.subtract, ALU.mult),
                              waits=[trow, ta, t_x2w, t_out_prev])
                ts2 = self.op("vector", lambda e, o=x2r, a=x2r, b=xrow[par]:
                              e.tensor_add(o, a, b))
                t_rowcons[par] = ts2
                t_x2w = self.dma("sync",
                                 io["x2_s"][(ch * 4 + r) * 128:(ch * 4 + r + 1) * 128, :],
                                 x2r, "sx2", waits=[ts2])
                tn2 = self.ln_row(x2r, lnt[par], stats, mv, sc1, waits=[ts2])
                ln2v = ln2_fm.rearrange("p (dt t) -> p dt t", t=512)
                t_lnfm = self.transpose_tm2fm(
                    lnt[par],
                    lambda grp, g, r=r, v=ln2v: v[:, grp:grp + g, r * 128:(r + 1) * 128],
                    DT, pe_waits=[tn2],
                    dve_waits=([pe_guard] if r == 0 else ()))

            for tsub in range(4):
                for dt in range(DT):
                    self.op("tensor",
                            lambda e, o=psD[:, tsub * 8: tsub * 8 + 8],
                            l=ln2_fm[:, dt * 512 + tsub * 128: dt * 512 + (tsub + 1) * 128],
                            r=wrTb[:, dt * 8:(dt + 1) * 8],
                            st=(dt == 0), sp=(dt == DT - 1):
                            e.matmul(o, l, r, start=st, stop=sp),
                            waits=(([t_lnfm] + t_small + [self.tok("dve")])
                                   if (tsub == 0 and dt == 0) else ()),
                            sem="pe", inc=(1 if (dt == DT - 1 and tsub == 3) else 0))
            t_lg = self.tok("pe")
            for dt in range(DT):
                self.op("tensor",
                        lambda e, o=psC[:, 0:512],
                        l=aTb[:, dt * 128:(dt + 1) * 128],
                        r=ln2_fm[:, dt * 512:(dt + 1) * 512],
                        st=(dt == 0), sp=(dt == DT - 1):
                        e.matmul(o, l, r, start=st, stop=sp),
                        waits=([prev_wt3] if dt == 0 else ()),
                        sem="pe", inc=(1 if dt == DT - 1 else 0))
            t_h = self.tok("pe")
            th_ev = self.op("vector", lambda e, o=hsb, i=psC[:, 0:512]:
                            e.tensor_copy(o, i), waits=[t_h])
            t_wt = {}
            for tsub in range(4):
                lg = psD[:, tsub * 8: tsub * 8 + 8]
                self.op("vector", lambda e, o=rt[:, 0:1], i=lg:
                        e.reduce_max(o, i, AX.X), waits=[t_lg])
                self.op("vector", lambda e, o=rt[:, 8:16], i=lg, m=rt[:, 0:1]:
                        e.tensor_scalar(o, i, m, None, ALU.is_equal))
                self.op("vector", lambda e, o=rt[:, 16:24], a=rt[:, 8:16], b=lg:
                        e.scalar_tensor_tensor(o, a, -1e30, b, ALU.mult, ALU.add))
                self.op("vector", lambda e, o=rt[:, 1:2], i=rt[:, 16:24]:
                        e.reduce_max(o, i, AX.X))
                self.op("vector", lambda e, o=rt[:, 24:32], i=rt[:, 16:24], m=rt[:, 1:2]:
                        e.tensor_scalar(o, i, m, None, ALU.is_equal))
                tdv = self.op("vector", lambda e, o=rt[:, 2:3], a=rt[:, 1:2], b=rt[:, 0:1]:
                              e.tensor_sub(o, a, b))
                te = self.op("scalar", lambda e, o=rt[:, 3:4], i=rt[:, 2:3]:
                             e.activation(o, i, AF.Exp, bias=self.b_consts[:, 2:3],
                                          scale=1.0), waits=[tdv])
                self.op("vector", lambda e, o=rt[:, 4:5], i=rt[:, 3:4]:
                        e.tensor_scalar(o, i, 1.0, None, ALU.add), waits=[te])
                self.op("vector", lambda e, o=rt[:, 5:6], i=rt[:, 4:5]:
                        e.reciprocal(o, i))
                self.op("vector", lambda e, o=rt[:, 6:7], i=rt[:, 5:6]:
                        e.tensor_scalar(o, i, 2.0, None, ALU.mult))
                self.op("vector", lambda e, o=rt[:, 7:8], i=rt[:, 6:7]:
                        e.tensor_scalar(o, i, -1.0, 2.0, ALU.mult, ALU.add))
                self.op("vector", lambda e, o=wrow, a=rt[:, 8:16], m=rt[:, 6:7]:
                        e.tensor_scalar(o, a, m, None, ALU.mult))
                self.op("vector", lambda e, o=wrow, a=rt[:, 24:32], m=rt[:, 7:8], b=wrow:
                        e.scalar_tensor_tensor(o, a, m, b, ALU.mult, ALU.add))
                twv = self.op("vector", lambda e, o=wrow16, i=wrow:
                              e.tensor_copy(o, i))
                tt = self.op("tensor",
                             lambda e, o=self.psC16[0:8, tsub * 128:(tsub + 1) * 128],
                             i=wrow16: e.transpose(o, i, self.b_ident),
                             waits=[twv, th_ev], sem="pe", inc=1)
                t_wt[tsub] = self.op("vector",
                                     lambda e, o=wtsb[0:8, tsub * 128:(tsub + 1) * 128],
                                     i=self.psC16[0:8, tsub * 128:(tsub + 1) * 128]:
                                     e.tensor_copy(o, i), waits=[tt])
            prev_wt3 = t_wt[3]
            trep = self.op("tensor", lambda e, o=psB[:, 0:512], l=self.b_rmat[0:8, :],
                           r=wtsb[0:8, :]: e.matmul(o, l, r, start=True, stop=True),
                           waits=[t_wt[3], prev_hw], sem="pe", inc=1)
            t_hw = self.op("vector", lambda e, o=wrepb, a=hsb, b=psB[:, 0:512]:
                           e.tensor_tensor(o, a, b, ALU.mult), waits=[trep])
            prev_hw = t_hw

            t_ffev = self.tok("dve")
            for oc in range(DC):
                for dg in range(0, DT, 4):
                    wpar = (dg // 4) % 2
                    wsem = "w%d" % wpar
                    wv = wbuf[wpar].rearrange("p (d o) -> p d o", d=4)
                    if ch == 0:
                        tw = self.dma("gpsimd", wv,
                                      io["wbT"][oc, dg:dg + 4].rearrange("d p o -> p d o"),
                                      "wg%d" % wpar, waits=[t_wcons[wpar], t_wcache[wpar]])
                        t_wcache[wpar] = self.dma(
                            "sync", io["wb_c"][oc, dg:dg + 4].rearrange("d p o -> p d o"),
                            wv, "swc%d" % wpar, waits=[tw])
                    else:
                        tw = self.dma("gpsimd", wv,
                                      io["wb_c"][oc, dg:dg + 4].rearrange("d p o -> p d o"),
                                      "wg%d" % wpar, waits=[t_wcons[wpar]] + t_ccache)
                    for di in range(4):
                        dt = dg + di
                        for tsub in range(4):
                            self.op("tensor",
                                    lambda e, o=psA[:, tsub * 512:(tsub + 1) * 512],
                                    l=ln2_fm[:, dt * 512 + tsub * 128: dt * 512 + (tsub + 1) * 128],
                                    r=wbuf[wpar][:, di * 512:(di + 1) * 512],
                                    st=(dt == 0):
                                    e.matmul(o, l, r, start=st, stop=False),
                                    waits=(([tw] + ([t_ffev, t_hw] if dt == 0 else []))
                                           if (tsub == 0 and di == 0) else ()),
                                    sem="pe", inc=(1 if (tsub == 3 and di == 3) else 0))
                    t_wcons[wpar] = self.tok("pe")
                for tsub in range(4):
                    self.op("tensor",
                            lambda e, o=psA[:, tsub * 512:(tsub + 1) * 512],
                            l=wrepb[:, tsub * 128:(tsub + 1) * 128],
                            r=bmTb[:, oc * 512:(oc + 1) * 512]:
                            e.matmul(o, l, r, start=False, stop=True),
                            sem="pe", inc=(1 if tsub == 3 else 0))
                tmm = self.tok("pe")
                bp = oc % 2
                tbias = self.dma("sync", biasb[:, bp * 512:(bp + 1) * 512],
                                 io["biasv"][:, oc * 512:(oc + 1) * 512]
                                 .to_broadcast([128, 512]),
                                 "sb%d" % bp, waits=[t_biascons[bp]])
                for tsub in range(4):
                    self.op("vector",
                            lambda e, o=ffn_out[:, tsub * D + oc * 512: tsub * D + (oc + 1) * 512],
                            a=psA[:, tsub * 512:(tsub + 1) * 512],
                            b=biasb[:, bp * 512:(bp + 1) * 512]:
                            e.tensor_tensor(o, a, b, ALU.add),
                            waits=[tmm, tbias])
                t_biascons[bp] = self.tok("dve")
                t_ffev = self.tok("dve")

            for r in range(4):
                emit_final_row(ch, t_ffev, r)
            t_wo_last = self.tok("pe")
            t_evA = t_ffev
            if ch == 0:
                t_ccache = [self.tok("swc0"), self.tok("swc1")]
        self.release(m0)



def _prep_weights(w_qkv, w_o, Wb, bias, Wr, A, Bm, DT=32):
    D = DT * 128
    NH, NKV = DT, DT // 4
    NOT = NH + 2 * NKV
    DC = DT // 4
    wqkvT = np.ascontiguousarray(
        w_qkv.reshape(NOT, 128, DT, 128).transpose(0, 3, 2, 1)).reshape(NOT, 128, D)
    woT = np.ascontiguousarray(
        w_o.T.reshape(NH, 128, DC, 512).transpose(0, 2, 1, 3))
    wbT = np.ascontiguousarray(
        Wb.T.reshape(DT, 128, DC, 512).transpose(2, 0, 1, 3))
    aT = np.ascontiguousarray(A.reshape(128, D).T.reshape(DT, 128, 128))
    bmT = np.ascontiguousarray(
        Bm.transpose(0, 2, 1).reshape(128, D).reshape(128, DC, 512).transpose(1, 0, 2))
    wrT = np.ascontiguousarray(Wr.T.reshape(DT, 128, 8))
    biasv = np.ascontiguousarray(bias.reshape(1, D)).astype(np.float32)
    return dict(wqkvT=wqkvT.astype(np.float32), woT=woT.astype(np.float32),
                wbT=wbT.astype(np.float32), aT=aT.astype(np.float32),
                bmT=bmT.astype(np.float32), wrT=wrT.astype(np.float32),
                biasv=biasv)


def _rope_tables(pos, hd=128):
    inv = (1.0 / ROPE_BASE ** (np.arange(0, hd, 2, dtype=np.float32) / hd))
    fr = pos.astype(np.float32)[None, :] * inv[:, None].astype(np.float32)
    c, s = np.cos(fr).astype(np.float32), np.sin(fr).astype(np.float32)
    cos = np.concatenate([c, c], axis=0)
    sin = np.concatenate([-s, s], axis=0)
    return np.ascontiguousarray(cos), np.ascontiguousarray(sin)


def _mask_tiles(parity):
    t = np.arange(TS)
    m = np.zeros((4, 128, TS), dtype=np.float32)
    m[0] = (np.arange(0, 128)[:, None] <= t[None, :])
    m[1] = (np.arange(128, 256)[:, None] <= t[None, :])
    m[2] = m[3] = 1.0 if parity == 1 else 0.0
    return np.ascontiguousarray(m.transpose(1, 0, 2).reshape(128, 4 * TS)).astype(BF)


def _core_inputs(xseq, wts, parity, nslot=16, dtiles=32):
    NS = nslot
    D = dtiles * 128
    perm = np.arange(NS)
    if parity == 1:
        perm = perm.reshape(-1, 2)[:, ::-1].reshape(-1)
    xp = np.ascontiguousarray(
        np.asarray(xseq).reshape(NS, TS, D)[perm].reshape(NS * TS, D)).astype(np.float32)
    slotpos = (perm[:, None] * TS + np.arange(TS)[None, :]).reshape(-1)
    kcos, ksin = _rope_tables(slotpos)
    ownpos = slotpos.reshape(NS, TS)[0::2].reshape(-1)
    qcos, qsin = _rope_tables(ownpos)
    consts = np.zeros((128, 8), dtype=np.float32)
    consts[:, 0] = -SHIFT
    consts[:, 1] = EPS
    consts[:, 2] = 0.0
    rmat = np.zeros((8, 128), dtype=np.float32)
    for e_ in range(8):
        rmat[e_, e_ * 16:(e_ + 1) * 16] = 1.0
    d = dict(xp=xp, qcos=qcos, qsin=qsin, kcos=kcos, ksin=ksin,
             masks=_mask_tiles(parity),
             ident=np.eye(128).astype(BF),
             onesb=np.ones((128, 128)).astype(BF),
             rmat=rmat.astype(BF),
             consts=consts,
             pflag=np.array([[parity]], dtype=np.int32))
    d.update(wts)
    return d


def _own_global(parity, nslot):
    perm = np.arange(nslot)
    if parity == 1:
        perm = perm.reshape(-1, 2)[:, ::-1].reshape(-1)
    return perm[0::2]


def kernel(L0_attn_x, L0_attn_w_qkv, L0_attn_w_o, L0_ffn_Wb, L0_ffn_bias,
           L0_ffn_Wr, L0_ffn_A, L0_ffn_Bm):
    NSLOT, DTILES = 16, 32
    B = L0_attn_x.shape[0]
    D = DTILES * 128
    x = np.asarray(L0_attn_x, dtype=np.float32)
    wts = _prep_weights(np.asarray(L0_attn_w_qkv, np.float32),
                        np.asarray(L0_attn_w_o, np.float32),
                        np.asarray(L0_ffn_Wb, np.float32),
                        np.asarray(L0_ffn_bias, np.float32),
                        np.asarray(L0_ffn_Wr, np.float32),
                        np.asarray(L0_ffn_A, np.float32),
                        np.asarray(L0_ffn_Bm, np.float32), DTILES)
    kb = KB(NSLOT, DTILES, groups=[[2 * b, 2 * b + 1] for b in range(B)])
    nc = kb.build()
    in_maps = []
    for c in range(2 * B):
        seq, parity = c // 2, c % 2
        in_maps.append(_core_inputs(x[seq], wts, parity, NSLOT, DTILES))
    res = run_bass_kernel_spmd(nc, in_maps, core_ids=list(range(2 * B)))
    outf = np.empty((B, NSLOT * TS, D), dtype=np.float32)
    for c in range(2 * B):
        seq, parity = c // 2, c % 2
        o = np.asarray(res.results[c]["out"]).reshape(NSLOT // 2, TS, D)
        for i, g in enumerate(_own_global(parity, NSLOT)):
            outf[seq, g * TS:(g + 1) * TS] = o[i]
    return outf


# version 16
# speedup vs baseline: 1.2376x; 1.2376x over previous
import math
import numpy as np
import ml_dtypes
from contextlib import ExitStack

import concourse.bass as bass
import concourse.mybir as mybir
from concourse.bass_utils import run_bass_kernel_spmd

F32 = mybir.dt.float32
BF16 = mybir.dt.bfloat16
AF = mybir.ActivationFunctionType
ALU = mybir.AluOpType
AX = mybir.AxisListType
BF = ml_dtypes.bfloat16

TS = 256
SHIFT = 12.0
SM_SCALE = 1.0 / math.sqrt(128.0)
EPS = 1e-5
ROPE_BASE = 500000.0

ENGINES = ("tensor", "vector", "scalar", "gpsimd", "sync")
T0 = ("qs", 0)


class KB:

    def __init__(self, nslot=16, dtiles=32, groups=None):
        self.groups = groups or [[0, 1], [2, 3], [4, 5], [6, 7]]
        self.NS = nslot
        self.OWN = nslot // 2
        self.DT = dtiles
        self.D = dtiles * 128
        self.NH = dtiles
        self.NKV = dtiles // 4
        self.NOT = self.NH + 2 * self.NKV
        self.DC = dtiles // 4
        self.NSUB = nslot // 4
        self.NCH = self.OWN // 2
        self.S = nslot * TS
        self.OWNT = self.OWN * TS

        self.cmds = {e: [] for e in ENGINES}
        self.cnt = {}
        self.maxw = {e: {} for e in ENGINES}
        self.sems = {}

    def _w(self, eng, waits):
        out = []
        mw = self.maxw[eng]
        for s, v in waits:
            if v <= 0 or mw.get(s, 0) >= v:
                continue
            mw[s] = v
            out.append((s, v))
        return out

    def op(self, eng, fn, waits=(), sem=None, inc=0):
        auto = ()
        if eng in ("vector", "scalar"):
            selfsem = "dve" if eng == "vector" else "act"
            c = self.cnt.get(selfsem, 0)
            if c:
                auto = ((selfsem, c),)
            if sem is None:
                sem, inc = selfsem, 1
        extra = [self.t_smalls] if hasattr(self, "t_smalls") else []
        wl = self._w(eng, list(auto) + extra + list(waits))
        if inc:
            self.cnt[sem] = self.cnt.get(sem, 0) + inc
            tok = (sem, self.cnt[sem])
        else:
            tok = None
        sems = self.sems

        def run(e, wl=wl, fn=fn, sem=sem, inc=inc):
            for s, v in wl:
                e.wait_ge(sems[s], v)
            inst = fn(e)
            if inc:
                inst.then_inc(sems[sem], inc)
        self.cmds[eng].append(run)
        return tok

    def dma(self, eng, out, in_, sem, waits=()):
        return self.op(eng, lambda e, o=out, i=in_: e.dma_start(out=o, in_=i),
                       waits=waits, sem=sem, inc=16)

    def tok(self, sem):
        return (sem, self.cnt.get(sem, 0))

    def barrier_waits(self):
        return [(s, c) for s, c in self.cnt.items()]

    def emit(self, block):
        for name in ENGINES:
            cmds = self.cmds[name]

            def runner(eng, cmds=cmds):
                for f in cmds:
                    f(eng)
            getattr(block, name)(runner)

    def alloc16(self, name, cols):
        off = self._ab
        self._ab += ((cols * 2 + 63) // 64) * 64
        assert self._ab <= self.AR_BYTES, f"SBUF overflow at {name}: {self._ab}"
        return self.ar16[:, off // 2: off // 2 + cols]

    def alloc32(self, name, cols):
        off = self._ab
        self._ab += ((cols * 4 + 63) // 64) * 64
        assert self._ab <= self.AR_BYTES, f"SBUF overflow at {name}: {self._ab}"
        return self.ar32[:, off // 4: off // 4 + cols]

    def mark(self):
        return self._ab

    def release(self, m):
        self._ab = m

    def build(self):
        nc = bass.Bass("TRN2", target_bir_lowering=False)
        self.nc = nc
        NS, DT, D = self.NS, self.DT, self.D
        NH, NKV, NOT, DC = self.NH, self.NKV, self.NOT, self.DC
        S, OWN, OWNT = self.S, self.OWN, self.OWNT

        io = {}
        io["xp"] = nc.dram_tensor("xp", [S, D], F32, kind="ExternalInput").ap()
        io["wqkvT"] = nc.dram_tensor("wqkvT", [NOT, 128, D], F32, kind="ExternalInput").ap()
        io["woT"] = nc.dram_tensor("woT", [NH, DC, 128, 512], F32, kind="ExternalInput").ap()
        io["wbT"] = nc.dram_tensor("wbT", [DC, DT, 128, 512], F32, kind="ExternalInput").ap()
        io["aT"] = nc.dram_tensor("aT", [DT, 128, 128], F32, kind="ExternalInput").ap()
        io["bmT"] = nc.dram_tensor("bmT", [DC, 128, 512], F32, kind="ExternalInput").ap()
        io["wrT"] = nc.dram_tensor("wrT", [DT, 128, 8], F32, kind="ExternalInput").ap()
        io["biasv"] = nc.dram_tensor("biasv", [1, D], F32, kind="ExternalInput").ap()
        io["qcos"] = nc.dram_tensor("qcos", [128, OWNT], F32, kind="ExternalInput").ap()
        io["qsin"] = nc.dram_tensor("qsin", [128, OWNT], F32, kind="ExternalInput").ap()
        io["kcos"] = nc.dram_tensor("kcos", [128, S], F32, kind="ExternalInput").ap()
        io["ksin"] = nc.dram_tensor("ksin", [128, S], F32, kind="ExternalInput").ap()
        io["masks"] = nc.dram_tensor("masks", [128, 4 * TS], BF16, kind="ExternalInput").ap()
        io["ident"] = nc.dram_tensor("ident", [128, 128], BF16, kind="ExternalInput").ap()
        io["onesb"] = nc.dram_tensor("onesb", [128, 128], BF16, kind="ExternalInput").ap()
        io["rmat"] = nc.dram_tensor("rmat", [8, 128], BF16, kind="ExternalInput").ap()
        io["consts"] = nc.dram_tensor("consts", [128, 8], F32, kind="ExternalInput").ap()
        io["pflag"] = nc.dram_tensor("pflag", [1, 1], mybir.dt.int32, kind="ExternalInput").ap()
        io["out"] = nc.dram_tensor("out", [OWNT, D], F32, kind="ExternalOutput").ap()

        io["k_s"] = nc.dram_tensor("k_s", [NKV, 128, S], BF16).ap()
        io["v_s"] = nc.dram_tensor("v_s", [S, NKV, 128], BF16).ap()
        io["kc_in"] = nc.dram_tensor("kc_in", [S // 1024, NKV, 128, 1024], BF16).ap()
        io["vc_in"] = nc.dram_tensor("vc_in", [S, NKV, 128], BF16).ap()
        io["kg_s"] = nc.dram_tensor("kg_s", [2, NKV, 128, 1024], BF16).ap()
        io["vg_s"] = nc.dram_tensor("vg_s", [2, 1024, NKV, 128], BF16).ap()
        io["q_s"] = nc.dram_tensor("q_s", [OWN, NH, 128, TS], BF16).ap()
        io["ao_s"] = nc.dram_tensor("ao_s", [NH, 128, OWNT], BF16).ap()
        io["x2_s"] = nc.dram_tensor("x2_s", [OWNT, D], F32).ap()
        io["wqkv_c"] = nc.dram_tensor("wqkv_c", [NOT, 128, D], BF16).ap()
        io["wo_c"] = nc.dram_tensor("wo_c", [NH, DC, 128, 512], BF16).ap()
        io["wb_c"] = nc.dram_tensor("wb_c", [DC, DT, 128, 512], BF16).ap()
        io["rc_s"] = nc.dram_tensor("rc_s", [OWN * NKV, 1024], F32).ap()
        self.io = io

        with ExitStack() as st:
            e = st.enter_context
            self.AR_BYTES = 207 * 1024
            arena = e(nc.sbuf_tensor("arena", [128, self.AR_BYTES // 2], BF16))
            self.ar16 = arena.ap()
            self.ar32 = arena.bitcast(F32).ap()
            self.ar32i = arena.bitcast(mybir.dt.int32).ap()
            self._ab = 0

            psA = e(nc.psum_tensor("psA", [128, 2048], F32))
            psB = e(nc.psum_tensor("psB", [128, 1024], F32))
            psC = e(nc.psum_tensor("psC", [128, 512], F32))
            psD = e(nc.psum_tensor("psD", [128, 512], F32))
            self.psA, self.psB = psA.ap(), psB.ap()
            self.psC, self.psD = psC.ap(), psD.ap()
            self.psC16 = psC.bitcast(BF16).ap()
            self.psD16 = psD.bitcast(BF16).ap()

            for sname in ("pe", "act", "dve", "qs", "w0", "w1",
                          "k0", "k1", "v0", "v1", "qq", "a0", "a1",
                          "x0", "x1", "sout", "swc0", "swc1", "sx2", "wg0", "wg1", "sb0", "sb1", "cc", "szr", "smg",
                          "skv0", "skv1", "svs", "sao0", "sao1", "src", "srp"):
                self.sems[sname] = e(nc.semaphore(sname))

            self._tpose_ev = {0: T0, 1: T0}
            self._tpose_ev2 = None
            self.b_onesf = self.alloc32("onesf", 128)
            self.b_ident = self.alloc16("ident", 128)
            self.b_ones = self.alloc16("ones", 128)
            self.b_rmat = self.alloc16("rmat", 128)
            self.b_masks = self.alloc16("masks", 4 * TS)
            self.b_consts = self.alloc32("consts", 16)
            self.b_pf = self.ar32i[:, (self._ab // 4):(self._ab // 4) + 1]
            self._ab += 64

            self.dma("sync", self.b_ident, io["ident"], "qs")
            self.dma("sync", self.b_ones[:, 0:128], io["onesb"], "qs")
            self.dma("sync", self.b_rmat[0:8, :], io["rmat"], "qs")
            self.dma("sync", self.b_masks, io["masks"], "qs")
            self.dma("sync", self.b_consts[:, 0:8], io["consts"], "qs")
            self.dma("sync", self.b_pf[0:1, 0:1], io["pflag"], "qs")
            self.t_smalls = self.tok("qs")
            self.op("vector", lambda e: e.tensor_copy(self.b_onesf, self.b_ones),
                    waits=[self.t_smalls])

            self.phaseA()
            self.phaseB()
            self.phaseC()

            fw = self._w("sync", [self.tok("sout")])
            sems = self.sems

            def fin(e, fw=fw):
                for s, v in fw:
                    e.wait_ge(sems[s], v)
            self.cmds["sync"].append(fin)

            block = e(nc.Block())
            self.emit(block)
        return nc

    def ln_stats(self, xin, stats, mv, sc1, waits):
        D = self.D
        nchunk = (D + 511) // 512
        for c in range(nchunk):
            lo, hi = c * 512, min(D, (c + 1) * 512)
            self.op("vector", lambda e, a=stats[:, 6 * c:6 * c + 6], b=xin[:, lo:hi]:
                    e.bn_stats(a, b), waits=(waits if c == 0 else ()))
        self.op("vector", lambda e, a=mv[:, 0:2], b=stats[:, 0:6 * nchunk]:
                e.bn_aggr(a, b))
        tv = self.tok("dve")
        self.op("scalar", lambda e, o=sc1[:, 0:1], i=mv[:, 1:2]:
                e.activation(o, i, AF.Ln, bias=self.b_consts[:, 1:2], scale=1.0),
                waits=[tv])
        ta = self.op("scalar", lambda e, o=sc1[:, 1:2], i=sc1[:, 0:1]:
                     e.activation(o, i, AF.Exp, bias=self.b_consts[:, 2:3], scale=-0.5))
        return ta

    def ln_row(self, xin, dst_bf, stats, mv, sc1, waits):
        ta = self.ln_stats(xin, stats, mv, sc1, waits)
        return self.op("vector", lambda e, o=dst_bf, i=xin, m=mv[:, 0:1], r=sc1[:, 1:2]:
                       e.tensor_scalar(o, i, m, r, ALU.subtract, ALU.mult),
                       waits=[ta])

    def transpose_tm2fm(self, src_bf, dst_grp_fn, n_dt, pe_waits, dve_waits=()):
        t = None
        dw = list(dve_waits)
        evtok = self._tpose_ev
        for gi, grp in enumerate(range(0, n_dt, 8)):
            g = min(8, n_dt - grp)
            par = gi % 2
            ps = self.psC16 if par == 0 else self.psD16
            w = list(pe_waits) + [evtok[par]]
            for j in range(g):
                self.op("tensor", lambda e, o=ps[:, j * 128:(j + 1) * 128],
                        i=src_bf[:, (grp + j) * 128:(grp + j + 1) * 128]:
                        e.transpose(o, i, self.b_ident),
                        waits=(w if j == 0 else ()),
                        sem="pe", inc=(1 if j == g - 1 else 0))
            tp = self.tok("pe")
            dst3 = dst_grp_fn(grp, g)
            src3 = ps[:, 0:g * 128].rearrange("p (j c) -> p j c", c=128)
            t = self.op("vector", lambda e, o=dst3, i=src3: e.tensor_copy(o, i),
                        waits=[tp] + dw)
            dw = []
            evtok[par] = t
        return t

    def rope(self, ps, cosb, sinb, out_bf, width, waits):
        swp, t1 = self._rope_swp, self._rope_t1
        self.op("vector", lambda e, o=swp[0:64, 0:width], i=ps[64:128, 0:width]:
                e.tensor_copy(o, i), waits=waits)
        self.op("vector", lambda e, o=swp[64:128, 0:width], i=ps[0:64, 0:width]:
                e.tensor_copy(o, i))
        self.op("vector", lambda e, o=t1[:, 0:width], i=ps[:, 0:width], c=cosb[:, 0:width]:
                e.tensor_mul(o, i, c))
        self.op("vector", lambda e, o=swp[:, 0:width], i=swp[:, 0:width], s=sinb[:, 0:width]:
                e.tensor_mul(o, i, s))
        return self.op("vector", lambda e, o=out_bf[:, 0:width], a=t1[:, 0:width],
                       b=swp[:, 0:width]: e.tensor_add(o, a, b))

    def phaseA(self):
        DT, D, NH, NKV, NOT = self.DT, self.D, self.NH, self.NKV, self.NOT
        NS = self.NS
        io = self.io
        m0 = self.mark()
        xrow = self.alloc32("xrowA", D)
        lnt = self.alloc16("lntA", D)
        stats = self.alloc32("statsA", 6 * ((D + 511) // 512))
        mv = self.alloc32("mvA", 4)
        sc1 = self.alloc32("sc1A", 4)
        ln1_fm = [self.alloc16("ln1fm", DT * 512) for _ in range(2)]
        wbuf = [self.alloc16("wbufA", D) for _ in range(2)]
        self._rope_swp = self.alloc32("ropeSwp", 512)
        self._rope_t1 = self.alloc32("ropeT1", 512)
        kq_bf = [self.alloc16("kqbf", 512) for _ in range(2)]
        vsb = self.alloc16("vsbA", 512)
        vtb = self.alloc16("vtbA", 512)
        rsc0 = self.alloc32("rscA", 512)
        rsc = [rsc0, rsc0]
        qcosb = self.alloc32("qcosA", 512)
        qsinb = self.alloc32("qsinA", 512)
        zt = self.alloc16("ztA", 2048)

        t_wcons = {0: T0, 1: T0}
        t_wcache = {0: T0, 1: T0}
        t_rowcons = T0
        t_lntcons = T0
        t_kqcons = {0: T0, 1: T0}
        t_vspill = T0
        t_psA = {0: T0, 1: T0}
        t_psB = {0: T0, 1: T0}
        t_vsb = T0
        t_qkv_pe = {0: T0, 1: T0}
        self._rope_prev = {}
        t_rope_last = T0
        t_lnfm = {}
        self.t_qkvcache = []

        tz = self.op("vector", lambda e: e.memset(zt, 0.0))
        for j in range(1, NS, 2):
            for g in range(NKV):
                self.dma("sync", io["kc_in"][j // 4, g][:, (j % 4) * TS:(j % 4 + 1) * TS],
                         zt[:, 0:TS], "szr", waits=[tz])
            self.dma("sync",
                     io["vc_in"][j * TS:(j + 1) * TS].rearrange(
                         "(kt p) g h -> p kt g h", p=128),
                     zt[:, 0:2 * NKV * 128].rearrange(
                         "p (kt g h) -> p kt g h", g=NKV, h=128),
                     "szr", waits=[tz])
        t_zero = self.tok("szr")


        v2d = io["vc_in"].rearrange("s g h -> (s g) h")
        kgo = io["kg_s"].rearrange("r g p s -> (r g p) s")
        vgo = io["vg_s"].rearrange("r s g h -> (r s g) h")
        NG = NKV * 128

        def emit_collective(p):
            self.op("gpsimd",
                    lambda e, i=io["kc_in"][p].rearrange("g p s -> (g p) s"), o=kgo:
                    e.collective_compute("AllGather", ALU.bypass,
                                         self.groups, [i], [o]),
                    waits=[self.tok("skv0"), self.tok("skv1"),
                           self.tok("svs"), self.tok("smg"), t_zero],
                    sem="cc", inc=1)
            tcc = self.op("gpsimd",
                          lambda e, i=v2d[p * 1024 * NKV:(p + 1) * 1024 * NKV, :],
                          o=vgo:
                          e.collective_compute("AllGather", ALU.bypass,
                                               self.groups, [i], [o]),
                          sem="cc", inc=1)
            kr = [io["kg_s"][r].rearrange("g p (pr two t) -> g p pr two t",
                                          two=2, t=TS) for r in range(2)]
            vr = [io["vg_s"][r].rearrange("(pr two t) g h -> pr two t g h",
                                          two=2, t=TS) for r in range(2)]
            ksl = io["k_s"][:, :, p * 1024:(p + 1) * 1024].rearrange(
                "g p (pr two t) -> g p pr two t", two=2, t=TS)
            vsl = io["v_s"][p * 1024:(p + 1) * 1024].rearrange(
                "(pr two t) g h -> pr two t g h", two=2, t=TS)
            wl = self._w("gpsimd", [tcc])
            sems = self.sems
            pf = self.b_pf

            def merge(e, wl=wl, kr=kr, vr=vr, ksl=ksl, vsl=vsl, p=p):
                for s_, v_ in wl:
                    e.wait_ge(sems[s_], v_)
                with e.register("prx%d" % p) as r_:
                    e.reg_load(r_, pf[0:1, 0:1])
                    for cond, m in ((e.If_eq, 0), (e.Else, None)):
                        ctx = cond(r_, 0) if m is not None else cond()
                        my = 0 if m is not None else 1
                        with ctx:
                            e.dma_start(out=ksl[:, :, :, 0, :],
                                        in_=kr[my][:, :, :, 0, :]).then_inc(sems["smg"], 16)
                            e.dma_start(out=ksl[:, :, :, 1, :],
                                        in_=kr[1 - my][:, :, :, 0, :]).then_inc(sems["smg"], 16)
                            e.dma_start(out=vsl[:, 0],
                                        in_=vr[my][:, 0]).then_inc(sems["smg"], 16)
                            e.dma_start(out=vsl[:, 1],
                                        in_=vr[1 - my][:, 0]).then_inc(sems["smg"], 16)
            self.cmds["gpsimd"].append(merge)
            self.cnt["smg"] = self.cnt.get("smg", 0) + 64

        def emit_tables(p):
            self.dma("sync", qcosb, io["qcos"][:, p * 512:(p + 1) * 512], "qs",
                     waits=[t_rope_last])
            self.dma("sync", qsinb, io["qsin"][:, p * 512:(p + 1) * 512], "qs")
            return self.tok("qs")

        def emit_ln_row(p, r):
            nonlocal t_rowcons, t_lntcons
            base = p * 1024 + (r // 2) * 512 + (r % 2) * 128
            trow = self.dma("sync", xrow, io["xp"][base:base + 128, :],
                            "x0", waits=[t_rowcons])
            tn = self.ln_row(xrow, lnt, stats, mv, sc1, waits=[trow, t_lntcons])
            t_rowcons = tn
            v = ln1_fm[p % 2].rearrange("p (dt t) -> p dt t", t=512)
            tk = self.transpose_tm2fm(
                lnt,
                lambda grp, g, r=r, v=v: v[:, grp:grp + g, r * 128:(r + 1) * 128],
                DT, pe_waits=[tn],
                dve_waits=([t_qkv_pe[p % 2]] if r == 0 else ()))
            t_lntcons = self.tok("pe")
            return tk

        t_tab = emit_tables(0)
        for r in range(4):
            t_lnfm[0] = emit_ln_row(0, r)

        for p in range(self.NSUB):
            fm = ln1_fm[p % 2]
            pending = list(range(4)) if p + 1 < self.NSUB else []
            step = max(1, NOT // 4)
            for ot in range(NOT):
                if ot == 2 and p > 0:
                    emit_collective(p - 1)
                wpar = ot % 2
                if p == 0:
                    tw = self.dma("gpsimd", wbuf[wpar], io["wqkvT"][ot], "wg%d" % wpar,
                                  waits=[t_wcons[wpar], t_wcache[wpar]])
                    t_wcache[wpar] = self.dma("sync", io["wqkv_c"][ot], wbuf[wpar],
                                              "swc%d" % wpar, waits=[tw])
                else:
                    tw = self.dma("gpsimd", wbuf[wpar], io["wqkv_c"][ot], "wg%d" % wpar,
                                  waits=[t_wcons[wpar]] + self.t_qkvcache)
                is_q = ot < NH
                if is_q:
                    qpar = ot % 2
                    ps = self.psB[:, qpar * 512:(qpar + 1) * 512]
                    wps = t_psB
                    pspar = qpar
                else:
                    kv = ot - NH
                    g = kv % NKV
                    is_k = kv < NKV
                    pspar = ot % 2
                    ps = self.psA[:, pspar * 1024: pspar * 1024 + 512]
                    wps = t_psA
                for dt in range(DT):
                    self.op("tensor",
                            lambda e, o=ps, l=wbuf[wpar][:, dt * 128:(dt + 1) * 128],
                            r=fm[:, dt * 512:(dt + 1) * 512],
                            s=(dt == 0), z=(dt == DT - 1):
                            e.matmul(o, l, r, start=s, stop=z),
                            waits=([tw, t_lnfm[p], wps[pspar]] if dt == 0 else ()),
                            sem="pe", inc=(1 if dt == DT - 1 else 0))
                t_mm = self.tok("pe")
                t_wcons[wpar] = t_mm
                if is_q or is_k:
                    kqp = ot % 2
                    tec = self.op("scalar", lambda e, o=rsc[kqp][:, 0:512], i=ps:
                                  e.copy(o, i),
                                  waits=[t_mm, self._rope_prev.get(0, T0),
                                         self._rope_prev.get(1, T0)])
                    wps[pspar] = tec
                    tdone = self.rope(rsc[kqp], qcosb, qsinb, kq_bf[kqp], 512,
                                      [tec, t_kqcons[kqp], t_tab])
                    self._rope_prev[kqp] = tdone
                    t_rope_last = tdone
                    if is_q:
                        dst = io["q_s"][2 * p: 2 * p + 2, ot].rearrange("s p t -> p s t")
                    else:
                        dst = io["kc_in"][p, g].rearrange(
                            "p (pr two t) -> p pr two t", two=2, t=TS)[:, :, 0, :]
                    src3 = kq_bf[kqp].rearrange("p (s t) -> p s t", s=2)
                    t_kqcons[kqp] = self.dma("sync", dst, src3, "skv%d" % kqp,
                                             waits=[tdone])
                else:
                    tev = self.op("scalar", lambda e, o=vsb, i=ps:
                                  e.copy(o, i), waits=[t_mm, t_vsb])
                    wps[pspar] = tev
                    tvt = self.transpose_tm2fm(
                        vsb,
                        lambda grp, g_: vtb[:, grp * 128:(grp + g_) * 128]
                        .rearrange("p (j c) -> p j c", c=128),
                        4, pe_waits=[tev], dve_waits=[t_vspill])
                    t_vsb = tvt
                    for c in range(2):
                        row0 = (4 * p + 2 * c) * TS
                        dst = io["vc_in"][row0:row0 + TS, g, :].rearrange(
                            "(kt p) h -> p kt h", p=128)
                        src3 = vtb[:, c * 256:(c + 1) * 256].rearrange(
                            "p (kt h) -> p kt h", h=128)
                        t_vspill = self.dma("sync", dst, src3, "svs", waits=[tvt])
                t_qkv_pe[p % 2] = t_mm
                if pending and ot % step == step - 1:
                    t_lnfm[p + 1] = emit_ln_row(p + 1, pending.pop(0))
            if p == 0:
                self.t_qkvcache = [self.tok("swc0"), self.tok("swc1")]
            if p + 1 < self.NSUB:
                while pending:
                    t_lnfm[p + 1] = emit_ln_row(p + 1, pending.pop(0))
                t_tab = emit_tables(p + 1)
        emit_collective(self.NSUB - 1)

        self.release(m0)
        self.tA_end = self.barrier_waits()

    def phaseB(self):
        NKV, OWN = self.NKV, self.OWN
        io = self.io
        m0 = self.mark()
        qb = self.alloc16("qB", 1024)
        kb = [self.alloc16("kB", 1024) for _ in range(2)]
        vb = [self.alloc16("vB", 1024) for _ in range(2)]
        ptr = [self.alloc16("ptR", 1024) for _ in range(4)]
        ptm = [self.alloc16("ptM", 1024) for _ in range(2)]
        recip = self.alloc32("recipB", 2048)
        rep = self.alloc32("repB", 1024)
        aob = [self.alloc16("aoB", 1024) for _ in range(2)]
        psA, psB, psC, psD = self.psA, self.psB, self.psC, self.psD

        bar = list(self.tA_end)
        first = True
        t_qcons = T0
        t_kcons = {0: T0, 1: T0}
        t_vcons = {0: T0, 1: T0}
        t_aocons = {0: T0, 1: T0}
        t_pe_ao = T0
        t_ao_ev = T0
        prev_exp_last = T0

        for s in range(OWN):
            kslots = 2 * s + 2
            nkt = 2 * kslots
            nch = (kslots + 3) // 4
            for g in range(NKV):
                w0 = bar if first else []
                src = io["q_s"][s, 4 * g: 4 * g + 4].rearrange("h p t -> p h t")
                qb3 = qb.rearrange("p (h t) -> p h t", h=4)
                tq = self.dma("scalar", qb3, src, "qq", waits=list(w0) + [t_qcons])
                tkl, tvl = {}, {}

                def load_chunk(c, extra):
                    cp = c % 2
                    span = min(1024, kslots * TS - c * 1024)
                    tkl[c] = self.dma("scalar", kb[cp][:, 0:span],
                                      io["k_s"][g][:, c * 1024: c * 1024 + span],
                                      "k%d" % cp,
                                      waits=list(extra) + [t_kcons[cp]])
                    tvl[c] = self.dma("scalar",
                                      vb[cp][:, 0:span].rearrange(
                                          "p (kt h) -> p kt h", h=128),
                                      io["v_s"][c * 1024: c * 1024 + span, g, :]
                                      .rearrange("(kt p) h -> p kt h", p=128),
                                      "v%d" % cp, waits=[t_vcons[cp]])

                for c in range(min(2, nch)):
                    load_chunk(c, w0 if first else [])
                first = False
                t_exp, t_pv = {}, {}

                def emit_qk(kt):
                    c, cp = kt // 8, (kt // 8) % 2
                    loc = kt % 8
                    sreg = psA[:, (kt % 2) * 1024:(kt % 2) * 1024 + 1024]
                    w = [tkl[c], tq]
                    if kt >= 2:
                        w.append(t_exp[kt - 2])
                    elif kt == 0:
                        w.append(prev_exp_last)
                    for hp in range(2):
                        self.op("tensor",
                                lambda e, o=sreg[:, hp * 512:(hp + 1) * 512],
                                l=kb[cp][:, loc * 128:(loc + 1) * 128],
                                r=qb[:, hp * 512:(hp + 1) * 512]:
                                e.matmul(o, l, r, start=True, stop=True),
                                waits=(w if hp == 0 else ()),
                                sem="pe", inc=(1 if hp == 1 else 0))
                    tqk = self.tok("pe")
                    wv = [tqk]
                    if kt >= 2:
                        wv.append(t_pv[kt - 2])
                    t_exp[kt] = self.op("scalar",
                                        lambda e, o=ptr[kt % 4], i=sreg:
                                        e.activation(o, i, AF.Exp,
                                                     bias=self.b_consts[:, 0:1],
                                                     scale=SM_SCALE),
                                        waits=wv)
                    if kt >= nkt - 4:
                        mi = kt - (nkt - 4)
                        mk = self.b_masks[:, mi * TS:(mi + 1) * TS] \
                            .rearrange("p (o t) -> p o t", o=1).to_broadcast([128, 4, TS])
                        pin = ptr[kt % 4].rearrange("p (h t) -> p h t", h=4)
                        pot = ptm[kt % 2].rearrange("p (h t) -> p h t", h=4)
                        wm = [t_exp[kt]]
                        if kt - 2 >= nkt - 4:
                            wm.append(t_pv[kt - 2])
                        t_exp[kt] = self.op("vector",
                                            lambda e, o=pot, i=pin, m=mk:
                                            e.tensor_tensor(o, i, m, ALU.mult),
                                            waits=wm)

                def emit_pvsum(kt):
                    cp, loc = (kt // 8) % 2, kt % 8
                    psrc = ptm[kt % 2] if kt >= nkt - 4 else ptr[kt % 4]
                    wpv = [t_exp[kt], tvl[kt // 8]]
                    if kt == 0:
                        wpv = wpv + [t_pe_ao_w, t_ao_ev_w]
                    for hp in range(2):
                        self.op("tensor",
                                lambda e, o=psB[:, hp * 512:(hp + 1) * 512],
                                l=vb[cp][:, loc * 128:(loc + 1) * 128],
                                r=psrc[:, hp * 512:(hp + 1) * 512],
                                st=(kt == 0), sp=(kt == nkt - 1):
                                e.matmul(o, l, r, start=st, stop=sp),
                                waits=(wpv if hp == 0 else ()), sem="pe", inc=0)
                    for hp in range(2):
                        ps_s = psC if hp == 0 else psD
                        self.op("tensor",
                                lambda e, o=ps_s[:, 0:512],
                                l=self.b_ones[:, 0:128],
                                r=psrc[:, hp * 512:(hp + 1) * 512],
                                st=(kt == 0), sp=(kt == nkt - 1):
                                e.matmul(o, l, r, start=st, stop=sp),
                                sem="pe", inc=(1 if hp == 1 else 0))
                    t_pv[kt] = self.tok("pe")
                    if loc == 7 and (kt // 8 + 2) < nch:
                        t_kcons[cp] = t_pv[kt]
                        t_vcons[cp] = t_pv[kt]
                        load_chunk(kt // 8 + 2, [])

                t_pe_ao_w, t_ao_ev_w = t_pe_ao, t_ao_ev
                for kt in range(nkt + 1):
                    if kt < nkt:
                        emit_qk(kt)
                    if kt >= 1:
                        emit_pvsum(kt - 1)
                prev_exp_last = t_exp[nkt - 1]
                t_qcons = t_pv[nkt - 1]
                for cp in range(2):
                    t_kcons[cp] = t_pv[nkt - 1]
                    t_vcons[cp] = t_pv[nkt - 1]
                self.op("scalar", lambda e, o=recip[:, 0:512], i=psC[:, 0:512]:
                        e.activation(o, i, AF.Ln, bias=self.b_consts[:, 2:3],
                                     scale=1.0), waits=[t_pv[nkt - 1]])
                self.op("scalar", lambda e, o=recip[:, 512:1024], i=psD[:, 0:512]:
                        e.activation(o, i, AF.Ln, bias=self.b_consts[:, 2:3],
                                     scale=1.0))
                te = self.op("scalar", lambda e, o=rep[:, 0:1024], i=recip[:, 0:1024]:
                             e.activation(o, i, AF.Exp, bias=self.b_consts[:, 2:3],
                                          scale=-1.0))
                ap = (s * NKV + g) % 2
                tao = self.op("vector",
                              lambda e, o=aob[ap], a=psB, b=rep:
                              e.tensor_tensor(o, a, b, ALU.mult),
                              waits=[te, t_aocons[ap]])
                t_ao_ev = tao
                t_pe_ao = t_pv[nkt - 1]
                dst = io["ao_s"][4 * g: 4 * g + 4, :, s * TS:(s + 1) * TS] \
                    .rearrange("h p t -> p h t")
                src3 = aob[ap].rearrange("p (h t) -> p h t", h=4)
                t_aocons[ap] = self.dma("sync", dst, src3, "sao%d" % ap, waits=[tao])

        self.release(m0)
        self.tB_end = self.barrier_waits()

    def phaseC(self):
        NH, DT, DC, D, NCH = self.NH, self.DT, self.DC, self.D, self.NCH
        io = self.io
        m0 = self.mark()
        aTb = self.alloc16("aTb", DT * 128)
        bmTb = self.alloc16("bmTb", DC * 512)
        wrTb = self.alloc16("wrTb", DT * 8)
        biasb = self.alloc32("biasC", 1024)
        bar = list(self.tB_end)
        self.dma("gpsimd", aTb.rearrange("p (dt er) -> p dt er", er=128), io["aT"].rearrange("dt p er -> p dt er"), "wg0",
                 waits=bar)
        self.dma("gpsimd", bmTb.rearrange("p (oc o) -> p oc o", o=512), io["bmT"].rearrange("oc p o -> p oc o"), "wg0")
        self.dma("gpsimd", wrTb.rearrange("p (dt o) -> p dt o", o=8), io["wrT"].rearrange("dt p o -> p dt o"), "wg0")
        t_small = [self.tok("wg0")]
        t_biascons = {0: T0, 1: T0}

        aob = self.alloc16("aoC", NH * 512)
        wo_out = self.alloc32("woout", 4 * D)
        ln2_fm = aob
        ffn_out = wo_out
        wbuf = [self.alloc16("wbufC", 2048) for _ in range(2)]
        xrow = [self.alloc32("xrowC", D) for _ in range(2)]
        lnt = [self.alloc16("lntC", D) for _ in range(2)]
        x2r = self.alloc32("x2rC", D)
        stats = self.alloc32("statsC", 6 * ((D + 511) // 512))
        mv = self.alloc32("mvC", 4)
        sc1 = self.alloc32("sc1C", 4)
        rt = self.alloc32("rtC", 64)
        wrow = self.alloc32("wrowC", 8)
        wrow16 = self.alloc16("wrow16", 8)
        wtsb = self.alloc16("wtsb", 512)
        wrepb = self.alloc16("wrepC", 512)
        hsb = self.alloc32("hsbC", 512)

        psA, psB, psC, psD = self.psA, self.psB, self.psC, self.psD
        t_evA = self.tok("act")
        t_wcons = {0: T0, 1: T0}
        t_wcache = {0: T0, 1: T0}
        t_aocons = {0: T0, 1: T0}
        t_rowcons = {0: T0, 1: T0}
        t_x2w = T0
        t_wo_last = T0
        t_ccache = []
        prev_wt3 = T0
        prev_hw = T0
        t_out_prev = T0

        def emit_final_row(ch, t_ffev, r):
            nonlocal t_x2w, t_out_prev
            if True:
                par = r % 2
                frow = ffn_out[:, r * D:(r + 1) * D]
                ta = self.ln_stats(frow, stats, mv, sc1, waits=[t_ffev])
                tx2 = self.dma("sync", xrow[par],
                               io["x2_s"][(ch * 4 + r) * 128:(ch * 4 + r + 1) * 128, :],
                               "x%d" % par, waits=[t_rowcons[par], t_x2w])
                ts1 = self.op("vector",
                              lambda e, o=x2r, i=frow, m=mv[:, 0:1], rs=sc1[:, 1:2]:
                              e.tensor_scalar(o, i, m, rs, ALU.subtract, ALU.mult),
                              waits=[tx2, ta, t_out_prev])
                ts2 = self.op("vector", lambda e, o=x2r, a=x2r, b=xrow[par]:
                              e.tensor_add(o, a, b))
                t_rowcons[par] = ts2
                t_out_prev = self.dma("sync",
                                      io["out"][(ch * 4 + r) * 128:(ch * 4 + r + 1) * 128, :],
                                      x2r, "sout", waits=[ts2])

        for ch in range(NCH):
            t_aocons[0] = self.dma(
                "sync", aob.rearrange("p (h t) -> p h t", t=512),
                io["ao_s"][:, :, ch * 512:(ch + 1) * 512].rearrange("h p t -> p h t"),
                "a0", waits=(bar if ch == 0 else []) + [t_aocons[0], t_wo_last])
            t_ao_all = [t_aocons[0]]
            t_ev = t_evA
            for dc in range(DC):
                for og in range(0, NH, 4):
                    wpar = (og // 4) % 2
                    wsem = "w%d" % wpar
                    wv = wbuf[wpar].rearrange("p (o d) -> p o d", o=4)
                    if ch == 0:
                        tw = self.dma("gpsimd", wv,
                                      io["woT"][og:og + 4, dc].rearrange("o p d -> p o d"),
                                      "wg%d" % wpar, waits=[t_wcons[wpar], t_wcache[wpar]])
                        t_wcache[wpar] = self.dma(
                            "sync", io["wo_c"][og:og + 4, dc].rearrange("o p d -> p o d"),
                            wv, "swc%d" % wpar, waits=[tw])
                    else:
                        tw = self.dma("gpsimd", wv,
                                      io["wo_c"][og:og + 4, dc].rearrange("o p d -> p o d"),
                                      "wg%d" % wpar, waits=[t_wcons[wpar]] + t_ccache)
                    for oi in range(4):
                        ot = og + oi
                        for tsub in range(4):
                            self.op("tensor",
                                    lambda e, o=psA[:, tsub * 512:(tsub + 1) * 512],
                                    l=aob[:, ot * 512 + tsub * 128: ot * 512 + (tsub + 1) * 128],
                                    r=wbuf[wpar][:, oi * 512:(oi + 1) * 512],
                                    st=(ot == 0), sp=(ot == NH - 1):
                                    e.matmul(o, l, r, start=st, stop=sp),
                                    waits=(([tw] + (t_ao_all + [t_ev] if ot == 0 else []))
                                           if (tsub == 0 and oi == 0) else ()),
                                    sem="pe",
                                    inc=(1 if (tsub == 3 and oi == 3) else 0))
                    t_wcons[wpar] = self.tok("pe")
                tmm = self.tok("pe")
                for tsub in range(4):
                    self.op("scalar",
                            lambda e, o=wo_out[:, tsub * D + dc * 512: tsub * D + (dc + 1) * 512],
                            i=psA[:, tsub * 512:(tsub + 1) * 512]:
                            e.copy(o, i), waits=[tmm])
                t_ev = self.tok("act")
            t_wo_last = self.tok("pe")

            pe_guard = self.tok("pe")
            t_lnfm = None
            for r in range(4):
                par = r % 2
                grow = (4 * ch + 2 * (r // 2)) * TS + (r % 2) * 128
                trow = self.dma("sync", xrow[par], io["xp"][grow:grow + 128, :],
                                "x%d" % par, waits=[t_rowcons[par]])
                worow = wo_out[:, r * D:(r + 1) * D]
                ta = self.ln_stats(worow, stats, mv, sc1, waits=[t_ev])
                ts1 = self.op("vector",
                              lambda e, o=x2r, i=worow, m=mv[:, 0:1], rs=sc1[:, 1:2]:
                              e.tensor_scalar(o, i, m, rs, ALU.subtract, ALU.mult),
                              waits=[trow, ta, t_x2w, t_out_prev])
                ts2 = self.op("vector", lambda e, o=x2r, a=x2r, b=xrow[par]:
                              e.tensor_add(o, a, b))
                t_rowcons[par] = ts2
                t_x2w = self.dma("sync",
                                 io["x2_s"][(ch * 4 + r) * 128:(ch * 4 + r + 1) * 128, :],
                                 x2r, "sx2", waits=[ts2])
                tn2 = self.ln_row(x2r, lnt[par], stats, mv, sc1, waits=[ts2])
                ln2v = ln2_fm.rearrange("p (dt t) -> p dt t", t=512)
                t_lnfm = self.transpose_tm2fm(
                    lnt[par],
                    lambda grp, g, r=r, v=ln2v: v[:, grp:grp + g, r * 128:(r + 1) * 128],
                    DT, pe_waits=[tn2],
                    dve_waits=([pe_guard] if r == 0 else ()))

            for tsub in range(4):
                for dt in range(DT):
                    self.op("tensor",
                            lambda e, o=psD[:, tsub * 8: tsub * 8 + 8],
                            l=ln2_fm[:, dt * 512 + tsub * 128: dt * 512 + (tsub + 1) * 128],
                            r=wrTb[:, dt * 8:(dt + 1) * 8],
                            st=(dt == 0), sp=(dt == DT - 1):
                            e.matmul(o, l, r, start=st, stop=sp),
                            waits=(([t_lnfm] + t_small + [self.tok("dve")])
                                   if (tsub == 0 and dt == 0) else ()),
                            sem="pe", inc=(1 if (dt == DT - 1 and tsub == 3) else 0))
            t_lg = self.tok("pe")
            for dt in range(DT):
                self.op("tensor",
                        lambda e, o=psC[:, 0:512],
                        l=aTb[:, dt * 128:(dt + 1) * 128],
                        r=ln2_fm[:, dt * 512:(dt + 1) * 512],
                        st=(dt == 0), sp=(dt == DT - 1):
                        e.matmul(o, l, r, start=st, stop=sp),
                        waits=([prev_wt3] if dt == 0 else ()),
                        sem="pe", inc=(1 if dt == DT - 1 else 0))
            t_h = self.tok("pe")
            th_ev = self.op("vector", lambda e, o=hsb, i=psC[:, 0:512]:
                            e.tensor_copy(o, i), waits=[t_h])
            t_wt = {}
            for tsub in range(4):
                lg = psD[:, tsub * 8: tsub * 8 + 8]
                self.op("vector", lambda e, o=rt[:, 0:1], i=lg:
                        e.reduce_max(o, i, AX.X), waits=[t_lg])
                self.op("vector", lambda e, o=rt[:, 8:16], i=lg, m=rt[:, 0:1]:
                        e.tensor_scalar(o, i, m, None, ALU.is_equal))
                self.op("vector", lambda e, o=rt[:, 16:24], a=rt[:, 8:16], b=lg:
                        e.scalar_tensor_tensor(o, a, -1e30, b, ALU.mult, ALU.add))
                self.op("vector", lambda e, o=rt[:, 1:2], i=rt[:, 16:24]:
                        e.reduce_max(o, i, AX.X))
                self.op("vector", lambda e, o=rt[:, 24:32], i=rt[:, 16:24], m=rt[:, 1:2]:
                        e.tensor_scalar(o, i, m, None, ALU.is_equal))
                tdv = self.op("vector", lambda e, o=rt[:, 2:3], a=rt[:, 1:2], b=rt[:, 0:1]:
                              e.tensor_sub(o, a, b))
                te = self.op("scalar", lambda e, o=rt[:, 3:4], i=rt[:, 2:3]:
                             e.activation(o, i, AF.Exp, bias=self.b_consts[:, 2:3],
                                          scale=1.0), waits=[tdv])
                self.op("vector", lambda e, o=rt[:, 4:5], i=rt[:, 3:4]:
                        e.tensor_scalar(o, i, 1.0, None, ALU.add), waits=[te])
                self.op("vector", lambda e, o=rt[:, 5:6], i=rt[:, 4:5]:
                        e.reciprocal(o, i))
                self.op("vector", lambda e, o=rt[:, 6:7], i=rt[:, 5:6]:
                        e.tensor_scalar(o, i, 2.0, None, ALU.mult))
                self.op("vector", lambda e, o=rt[:, 7:8], i=rt[:, 6:7]:
                        e.tensor_scalar(o, i, -1.0, 2.0, ALU.mult, ALU.add))
                self.op("vector", lambda e, o=wrow, a=rt[:, 8:16], m=rt[:, 6:7]:
                        e.tensor_scalar(o, a, m, None, ALU.mult))
                self.op("vector", lambda e, o=wrow, a=rt[:, 24:32], m=rt[:, 7:8], b=wrow:
                        e.scalar_tensor_tensor(o, a, m, b, ALU.mult, ALU.add))
                twv = self.op("vector", lambda e, o=wrow16, i=wrow:
                              e.tensor_copy(o, i))
                tt = self.op("tensor",
                             lambda e, o=self.psC16[0:8, tsub * 128:(tsub + 1) * 128],
                             i=wrow16: e.transpose(o, i, self.b_ident),
                             waits=[twv, th_ev], sem="pe", inc=1)
                t_wt[tsub] = self.op("vector",
                                     lambda e, o=wtsb[0:8, tsub * 128:(tsub + 1) * 128],
                                     i=self.psC16[0:8, tsub * 128:(tsub + 1) * 128]:
                                     e.tensor_copy(o, i), waits=[tt])
            prev_wt3 = t_wt[3]
            trep = self.op("tensor", lambda e, o=psB[:, 0:512], l=self.b_rmat[0:8, :],
                           r=wtsb[0:8, :]: e.matmul(o, l, r, start=True, stop=True),
                           waits=[t_wt[3], prev_hw], sem="pe", inc=1)
            t_hw = self.op("vector", lambda e, o=wrepb, a=hsb, b=psB[:, 0:512]:
                           e.tensor_tensor(o, a, b, ALU.mult), waits=[trep])
            prev_hw = t_hw

            t_ffev = self.tok("dve")
            for oc in range(DC):
                for dg in range(0, DT, 4):
                    wpar = (dg // 4) % 2
                    wsem = "w%d" % wpar
                    wv = wbuf[wpar].rearrange("p (d o) -> p d o", d=4)
                    if ch == 0:
                        tw = self.dma("gpsimd", wv,
                                      io["wbT"][oc, dg:dg + 4].rearrange("d p o -> p d o"),
                                      "wg%d" % wpar, waits=[t_wcons[wpar], t_wcache[wpar]])
                        t_wcache[wpar] = self.dma(
                            "sync", io["wb_c"][oc, dg:dg + 4].rearrange("d p o -> p d o"),
                            wv, "swc%d" % wpar, waits=[tw])
                    else:
                        tw = self.dma("gpsimd", wv,
                                      io["wb_c"][oc, dg:dg + 4].rearrange("d p o -> p d o"),
                                      "wg%d" % wpar, waits=[t_wcons[wpar]] + t_ccache)
                    for di in range(4):
                        dt = dg + di
                        for tsub in range(4):
                            self.op("tensor",
                                    lambda e, o=psA[:, tsub * 512:(tsub + 1) * 512],
                                    l=ln2_fm[:, dt * 512 + tsub * 128: dt * 512 + (tsub + 1) * 128],
                                    r=wbuf[wpar][:, di * 512:(di + 1) * 512],
                                    st=(dt == 0):
                                    e.matmul(o, l, r, start=st, stop=False),
                                    waits=(([tw] + ([t_ffev, t_hw] if dt == 0 else []))
                                           if (tsub == 0 and di == 0) else ()),
                                    sem="pe", inc=(1 if (tsub == 3 and di == 3) else 0))
                    t_wcons[wpar] = self.tok("pe")
                for tsub in range(4):
                    self.op("tensor",
                            lambda e, o=psA[:, tsub * 512:(tsub + 1) * 512],
                            l=wrepb[:, tsub * 128:(tsub + 1) * 128],
                            r=bmTb[:, oc * 512:(oc + 1) * 512]:
                            e.matmul(o, l, r, start=False, stop=True),
                            sem="pe", inc=(1 if tsub == 3 else 0))
                tmm = self.tok("pe")
                bp = oc % 2
                tbias = self.dma("sync", biasb[:, bp * 512:(bp + 1) * 512],
                                 io["biasv"][:, oc * 512:(oc + 1) * 512]
                                 .to_broadcast([128, 512]),
                                 "sb%d" % bp, waits=[t_biascons[bp]])
                for tsub in range(4):
                    self.op("vector",
                            lambda e, o=ffn_out[:, tsub * D + oc * 512: tsub * D + (oc + 1) * 512],
                            a=psA[:, tsub * 512:(tsub + 1) * 512],
                            b=biasb[:, bp * 512:(bp + 1) * 512]:
                            e.tensor_tensor(o, a, b, ALU.add),
                            waits=[tmm, tbias])
                t_biascons[bp] = self.tok("dve")
                t_ffev = self.tok("dve")

            for r in range(4):
                emit_final_row(ch, t_ffev, r)
            t_wo_last = self.tok("pe")
            t_evA = t_ffev
            if ch == 0:
                t_ccache = [self.tok("swc0"), self.tok("swc1")]
        self.release(m0)



def _prep_weights(w_qkv, w_o, Wb, bias, Wr, A, Bm, DT=32):
    D = DT * 128
    NH, NKV = DT, DT // 4
    NOT = NH + 2 * NKV
    DC = DT // 4
    wqkvT = np.ascontiguousarray(
        w_qkv.reshape(NOT, 128, DT, 128).transpose(0, 3, 2, 1)).reshape(NOT, 128, D)
    woT = np.ascontiguousarray(
        w_o.T.reshape(NH, 128, DC, 512).transpose(0, 2, 1, 3))
    wbT = np.ascontiguousarray(
        Wb.T.reshape(DT, 128, DC, 512).transpose(2, 0, 1, 3))
    aT = np.ascontiguousarray(A.reshape(128, D).T.reshape(DT, 128, 128))
    bmT = np.ascontiguousarray(
        Bm.transpose(0, 2, 1).reshape(128, D).reshape(128, DC, 512).transpose(1, 0, 2))
    wrT = np.ascontiguousarray(Wr.T.reshape(DT, 128, 8))
    biasv = np.ascontiguousarray(bias.reshape(1, D)).astype(np.float32)
    return dict(wqkvT=wqkvT.astype(np.float32), woT=woT.astype(np.float32),
                wbT=wbT.astype(np.float32), aT=aT.astype(np.float32),
                bmT=bmT.astype(np.float32), wrT=wrT.astype(np.float32),
                biasv=biasv)


def _rope_tables(pos, hd=128):
    inv = (1.0 / ROPE_BASE ** (np.arange(0, hd, 2, dtype=np.float32) / hd))
    fr = pos.astype(np.float32)[None, :] * inv[:, None].astype(np.float32)
    c, s = np.cos(fr).astype(np.float32), np.sin(fr).astype(np.float32)
    cos = np.concatenate([c, c], axis=0)
    sin = np.concatenate([-s, s], axis=0)
    return np.ascontiguousarray(cos), np.ascontiguousarray(sin)


def _mask_tiles(parity):
    t = np.arange(TS)
    m = np.zeros((4, 128, TS), dtype=np.float32)
    m[0] = (np.arange(0, 128)[:, None] <= t[None, :])
    m[1] = (np.arange(128, 256)[:, None] <= t[None, :])
    m[2] = m[3] = 1.0 if parity == 1 else 0.0
    return np.ascontiguousarray(m.transpose(1, 0, 2).reshape(128, 4 * TS)).astype(BF)


def _core_inputs(xseq, wts, parity, nslot=16, dtiles=32):
    NS = nslot
    D = dtiles * 128
    perm = np.arange(NS)
    if parity == 1:
        perm = perm.reshape(-1, 2)[:, ::-1].reshape(-1)
    xp = np.ascontiguousarray(
        np.asarray(xseq).reshape(NS, TS, D)[perm].reshape(NS * TS, D)).astype(np.float32)
    slotpos = (perm[:, None] * TS + np.arange(TS)[None, :]).reshape(-1)
    kcos, ksin = _rope_tables(slotpos)
    ownpos = slotpos.reshape(NS, TS)[0::2].reshape(-1)
    qcos, qsin = _rope_tables(ownpos)
    consts = np.zeros((128, 8), dtype=np.float32)
    consts[:, 0] = -SHIFT
    consts[:, 1] = EPS
    consts[:, 2] = 0.0
    rmat = np.zeros((8, 128), dtype=np.float32)
    for e_ in range(8):
        rmat[e_, e_ * 16:(e_ + 1) * 16] = 1.0
    d = dict(xp=xp, qcos=qcos, qsin=qsin, kcos=kcos, ksin=ksin,
             masks=_mask_tiles(parity),
             ident=np.eye(128).astype(BF),
             onesb=np.ones((128, 128)).astype(BF),
             rmat=rmat.astype(BF),
             consts=consts,
             pflag=np.array([[parity]], dtype=np.int32))
    d.update(wts)
    return d


def _own_global(parity, nslot):
    perm = np.arange(nslot)
    if parity == 1:
        perm = perm.reshape(-1, 2)[:, ::-1].reshape(-1)
    return perm[0::2]


def kernel(L0_attn_x, L0_attn_w_qkv, L0_attn_w_o, L0_ffn_Wb, L0_ffn_bias,
           L0_ffn_Wr, L0_ffn_A, L0_ffn_Bm):
    NSLOT, DTILES = 16, 32
    B = L0_attn_x.shape[0]
    D = DTILES * 128
    x = np.asarray(L0_attn_x, dtype=np.float32)
    wts = _prep_weights(np.asarray(L0_attn_w_qkv, np.float32),
                        np.asarray(L0_attn_w_o, np.float32),
                        np.asarray(L0_ffn_Wb, np.float32),
                        np.asarray(L0_ffn_bias, np.float32),
                        np.asarray(L0_ffn_Wr, np.float32),
                        np.asarray(L0_ffn_A, np.float32),
                        np.asarray(L0_ffn_Bm, np.float32), DTILES)
    kb = KB(NSLOT, DTILES, groups=[[2 * b, 2 * b + 1] for b in range(B)])
    nc = kb.build()
    in_maps = []
    for c in range(2 * B):
        seq, parity = c // 2, c % 2
        in_maps.append(_core_inputs(x[seq], wts, parity, NSLOT, DTILES))
    res = run_bass_kernel_spmd(nc, in_maps, core_ids=list(range(2 * B)))
    outf = np.empty((B, NSLOT * TS, D), dtype=np.float32)
    for c in range(2 * B):
        seq, parity = c // 2, c % 2
        o = np.asarray(res.results[c]["out"]).reshape(NSLOT // 2, TS, D)
        for i, g in enumerate(_own_global(parity, NSLOT)):
            outf[seq, g * TS:(g + 1) * TS] = o[i]
    return outf
